# version 07
# baseline (speedup 1.0000x reference)
from contextlib import ExitStack

import numpy as np
import ml_dtypes
import concourse.bass as bass
import concourse.mybir as mybir
from concourse.bass_utils import run_bass_kernel_spmd

F32 = mybir.dt.float32
BF16 = mybir.dt.bfloat16
I32 = mybir.dt.int32
AF = mybir.ActivationFunctionType
ALU = mybir.AluOpType
AX = mybir.AxisListType

NT = 2048
TILES = 16
NCH = 32
D = 4096
SCALE_Q = 1.0 / float(np.sqrt(128.0))
NSLAB = 16
SLABW = 256
EPS = 1e-5


class Sem:
    def __init__(self, nc, st, name):
        self.h = st.enter_context(nc.semaphore(name))
        self.n = 0

    def inc(self, instr, amount=1):
        instr.then_inc(self.h, amount)
        self.n += amount
        return self.n


class DStream:

    def __init__(self, nc, st, name, r=2):
        self.sems = [Sem(nc, st, f"{name}{i}") for i in range(r)]
        self.r = r
        self.i = 0

    def issue(self, eng, out, in_, **kw):
        p = self.i % self.r
        if self.i >= self.r:
            eng.wait_ge(self.sems[p].h, 16 * (self.i // self.r))
        self.sems[p].inc(eng.dma_start(out, in_, **kw), 16)
        self.i += 1
        return self.i - 1

    def wait(self, eng, idx):
        p = idx % self.r
        eng.wait_ge(self.sems[p].h, 16 * (idx // self.r + 1))

    def wait_all_upto(self, eng, idx):
        for p in range(self.r):
            last = idx - ((idx - p) % self.r)
            if last >= 0:
                eng.wait_ge(self.sems[p].h, 16 * (last // self.r + 1))

    def wait_done(self, eng):
        for p in range(self.r):
            last = self.i - 1 - ((self.i - 1 - p) % self.r)
            if last >= 0:
                eng.wait_ge(self.sems[p].h, 16 * (last // self.r + 1))


def build_program(dbg=False):
    nc = bass.Bass("TRN2", target_bir_lowering=False)

    x_d = nc.dram_tensor("x", [NT, D], F32, kind="ExternalInput").ap()
    wq_d = nc.dram_tensor("wq", [D, D], F32, kind="ExternalInput").ap()
    wk_d = nc.dram_tensor("wk", [D, 128], F32, kind="ExternalInput").ap()
    wv_d = nc.dram_tensor("wv", [D, 128], F32, kind="ExternalInput").ap()
    bq_d = nc.dram_tensor("bq", [128, 32], F32, kind="ExternalInput").ap()
    bk_d = nc.dram_tensor("bk", [128], F32, kind="ExternalInput").ap()
    bv_d = nc.dram_tensor("bv", [128], F32, kind="ExternalInput").ap()
    wt_d = nc.dram_tensor("wt", [D, D], F32, kind="ExternalInput").ap()
    bb_d = nc.dram_tensor("bb", [D], F32, kind="ExternalInput").ap()
    rwt_d = nc.dram_tensor("rwt", [D, 15], F32, kind="ExternalInput").ap()
    rb_d = nc.dram_tensor("rb", [15], F32, kind="ExternalInput").ap()
    at_d = nc.dram_tensor("at", [D, 128], F32, kind="ExternalInput").ap()
    bt_d = nc.dram_tensor("bt", [128, D], F32, kind="ExternalInput").ap()
    em_d = nc.dram_tensor("emap", [15], I32, kind="ExternalInput").ap()
    y_d = nc.dram_tensor("y", [NT, D], F32, kind="ExternalOutput").ap()

    scr_qk = nc.dram_tensor("scr_qk", [TILES, 128, 256], BF16, kind="Internal").ap()
    scr_b = nc.dram_tensor("scr_b", [128], F32, kind="Internal").ap()
    pre2_d = nc.dram_tensor("pre2", [NT, D], BF16, kind="Internal").ap()

    if dbg:
        dbg_o = nc.dram_tensor("dbg_o", [TILES, 128, 128], BF16, kind="ExternalOutput").ap()
        dbg_x1 = nc.dram_tensor("dbg_x1", [TILES, 128, D], BF16, kind="ExternalOutput").ap()
        dbg_pr = nc.dram_tensor("dbg_pr", [TILES, 128, 15], F32, kind="ExternalOutput").ap()
        dbg_w = nc.dram_tensor("dbg_w", [TILES, 8, 128], BF16, kind="ExternalOutput").ap()
        dbg_hw = nc.dram_tensor("dbg_hw", [128, NT], BF16, kind="ExternalOutput").ap()
        dbg_wq = nc.dram_tensor("dbg_wq", [128, NCH * 384], BF16, kind="ExternalOutput").ap()
        dbg_br = nc.dram_tensor("dbg_br", [1, 384], BF16, kind="ExternalOutput").ap()
        dbg_qk = nc.dram_tensor("dbg_qk", [TILES, 128, 256], BF16, kind="ExternalOutput").ap()

    id128_h = nc.inline_tensor(np.eye(128, dtype=np.float32).astype(ml_dtypes.bfloat16), "c_id128")
    id15_h = nc.inline_tensor(np.eye(15, dtype=np.float32).astype(ml_dtypes.bfloat16), "c_id15")
    repl_np = np.zeros((128, D), np.float32)
    repl_np[np.arange(D) // 32, np.arange(D)] = 1.0
    repl_h = nc.inline_tensor(repl_np.astype(ml_dtypes.bfloat16), "c_repl")
    r16_np = np.zeros((8, 128), np.float32)
    r16_np[np.arange(128) // 16, np.arange(128)] = 1.0
    r16_h = nc.inline_tensor(r16_np.astype(ml_dtypes.bfloat16), "c_r16")
    iota_h = nc.inline_tensor(np.tile(np.arange(8, dtype=np.float32), (15, 1)), "c_iota")
    ones_h = nc.inline_tensor(np.ones((1, 128), np.float32).astype(ml_dtypes.bfloat16), "c_ones")

    st = ExitStack()
    with st:
        E = st.enter_context

        xT = E(nc.sbuf_tensor("xT", [128, NCH * NT], BF16))
        id128 = E(nc.sbuf_tensor("id128", [128, 128], BF16))
        id15 = E(nc.sbuf_tensor("id15", [15, 15], BF16))
        ones1 = E(nc.sbuf_tensor("ones1", [1, 128], BF16))
        brow = E(nc.sbuf_tensor("brow", [1, 384], BF16))
        r16 = E(nc.sbuf_tensor("r16", [8, 128], BF16))
        iota = E(nc.sbuf_tensor("iota", [15, 8], F32))
        emc = E(nc.sbuf_tensor("emc", [15, 1], I32))
        emf = E(nc.sbuf_tensor("emf", [15, 1], F32))
        ohb = E(nc.sbuf_tensor("ohb", [15, 8], BF16))
        rbc = E(nc.sbuf_tensor("rbc", [15, 1], F32))
        epsc = E(nc.sbuf_tensor("epsc", [128, 1], F32))
        hwT = E(nc.sbuf_tensor("hwT", [128, NT], BF16))

        pA = E(nc.psum_tensor("pA", [128, 512], F32))
        pB = E(nc.psum_tensor("pB", [128, 512], F32))
        pC = E(nc.psum_tensor("pC", [128, 512], F32))
        pD = E(nc.psum_tensor("pD", [128, 512], F32))
        pE = E(nc.psum_tensor("pE", [128, 1024], F32))
        pF = E(nc.psum_tensor("pF", [128, 1024], F32))

        sst = ExitStack()
        st.enter_context(sst)
        mk = lambda name: Sem(nc, sst, name)
        s_cl = mk("cl"); s_xl = mk("xl"); s_xtr = mk("xtr"); s_xev = mk("xev")
        s_wql = mk("wql"); s_wqr = mk("wqr"); s_bx = mk("bx")
        s_qkv = mk("qkv"); s_qke = mk("qke"); s_fl = mk("fl"); s_vt = mk("vt")
        s_vi = mk("vi"); s_sg = mk("sg"); s_ex = mk("ex"); s_nd = mk("nd")
        s_dv = mk("dv"); s_ln = mk("ln"); s_le = mk("le"); s_stt = mk("stt")
        s_nm = mk("nm"); s_x1 = mk("x1"); s_x1e = mk("x1e")
        s_b0m = mk("b0m"); s_b0e = mk("b0e"); s_lgt = mk("lgt"); s_pr = mk("pr")
        s_exr = mk("exr"); s_wct = mk("wct"); s_wce = mk("wce"); s_w8 = mk("w8")
        s_w8e = mk("w8e"); s_w128 = mk("w128"); s_hw = mk("hw")
        s_wl = mk("wl"); s_bl = mk("bl"); s_fm = mk("fm"); s_fe = mk("fe")
        s_fb = mk("fb"); s_sp = mk("sp"); s_p2 = mk("p2"); s_pcs = mk("pcs")
        s_pcn = mk("pcn"); s_out = mk("out"); s_ph = mk("ph"); s_cl2 = mk("cl2"); s_dbg = mk("dbg")
        s_vv = mk("vv"); s_bx1 = mk("bx1"); s_bx2 = mk("bx2"); s_b0d = mk("b0d"); s_xc = mk("xc")
        ds_x = DStream(nc, sst, "dsx")
        ds_wq = DStream(nc, sst, "dswq")
        ds_fo = DStream(nc, sst, "dsfo")
        ds_fi = DStream(nc, sst, "dsfi")
        ds_sl = DStream(nc, sst, "dssl")
        ds_bb = DStream(nc, sst, "dsbb")
        ds_sp = DStream(nc, sst, "dssp")
        ds_p2 = DStream(nc, sst, "dsp2")
        ds_y = DStream(nc, sst, "dsy")

        def vst(v, ins):
            s_vv.inc(ins)

        def vwt(v):
            v.wait_ge(s_vv.h, s_vv.n)

        xT3 = xT[:].rearrange("p (c n) -> p c n", c=NCH)

        PH = [0]

        def barrier(blk, guards):
            PH[0] += 1
            tgt = 5 * PH[0]
            for name in ("gpsimd", "sync", "vector", "scalar", "tensor"):
                def _f(eng, _g=guards.get(name, ()), _t=tgt):
                    for sem, val in _g:
                        eng.wait_ge(sem.h, val)
                    eng.nop().then_inc(s_ph.h, 1)
                    eng.wait_ge(s_ph.h, _t)
                getattr(blk, name)(_f)
            s_ph.n = tgt

        p01 = ExitStack(); st.enter_context(p01)
        wqkv = p01.enter_context(nc.sbuf_tensor("wqkv", [128, NCH * 384], BF16))
        p0 = ExitStack(); st.enter_context(p0)
        xrows = [p0.enter_context(nc.sbuf_tensor(f"xrows{i}", [128, D], BF16)) for i in range(2)]
        xf32 = p0.enter_context(nc.sbuf_tensor("xf32", [128, D], F32))
        wqst = [p0.enter_context(nc.sbuf_tensor(f"wqst{i}", [128, D], BF16)) for i in range(2)]
        wq3 = wqkv[:].rearrange("p (c n) -> p c n", c=NCH)
        wqcol = p0.enter_context(nc.sbuf_tensor("wqcol", [128, 128], F32))
        bqs = p0.enter_context(nc.sbuf_tensor("bqs", [128, 32], F32))
        bqc = p0.enter_context(nc.sbuf_tensor("bqc", [128, 1], F32))

        with nc.Block() as blk:
            @blk.sync
            def _(sy):
                s_cl.inc(sy.dma_start(id128[:], id128_h.ap()), 16)
                s_cl.inc(sy.dma_start(id15[:], id15_h.ap()), 16)
                s_cl.inc(sy.dma_start(ones1[:], ones_h.ap()), 16)
                s_cl.inc(sy.dma_start(r16[:], r16_h.ap()), 16)
                s_cl.inc(sy.dma_start(iota[:], iota_h.ap()), 16)
                s_cl.inc(sy.dma_start(emc[:], em_d.unsqueeze(1)), 16)
                s_cl.inc(sy.dma_start(rbc[:], rb_d.unsqueeze(1)), 16)
                s_cl.inc(sy.dma_start(bqs[:], bq_d), 16)
                for t in range(TILES):
                    if t >= 1:
                        sy.wait_ge(s_xc.h, t)
                    ds_x.issue(sy, xf32[:], x_d[t * 128:(t + 1) * 128, :])
                sy.wait_ge(s_bx.h, 1)
                s_bx1.inc(sy.dma_start(scr_b.unsqueeze(1), bqc[:]), 16)

            @blk.gpsimd
            def _(g):
                s_cl2.inc(g.dma_start(wq3[:, :, 128:256],
                                      wk_d.rearrange("(c p) n -> p c n", p=128)), 16)
                s_cl2.inc(g.dma_start(wq3[:, :, 256:384],
                                      wv_d.rearrange("(c p) n -> p c n", p=128)), 16)
                s_cl2.inc(g.dma_start(brow[0:1, 128:256], bk_d.unsqueeze(0)), 16)
                s_cl2.inc(g.dma_start(brow[0:1, 256:384], bv_d.unsqueeze(0)), 16)
                for c in range(NCH):
                    if c >= 2:
                        g.wait_ge(s_wqr.h, c - 1)
                    ds_wq.issue(g, wqst[c % 2][:], wq_d[c * 128:(c + 1) * 128, :])
                g.wait_ge(s_bx1.h, 16)
                s_bx2.inc(g.dma_start(brow[0:1, 0:128], scr_b.unsqueeze(0)), 16)

            @blk.vector
            def _(v):
                v.memset(epsc[:], EPS)
                v.wait_ge(s_cl.h, 128)
                s_bx.inc(v.reduce_sum(bqc[:], bqs[:], axis=AX.X), 1)

                def xcast(t):
                    ds_x.wait(v, t)
                    if t >= 2:
                        v.wait_ge(s_xtr.h, 4 * (t - 2) + 4)
                    s_xc.inc(v.tensor_copy(xrows[t % 2][:], xf32[:]))

                for t in range(TILES):
                    xcast(t)
                    for c in (2 * t, 2 * t + 1):
                        ds_wq.wait(v, c)
                        v.wait_ge(s_wqr.h, c)
                        vst(v, v.reduce_sum(
                            wqcol[:],
                            wqst[c % 2][:].rearrange("p (d h) -> p d h", h=32),
                            axis=AX.X))
                        vwt(v)
                        s_wqr.inc(v.tensor_copy(wq3[:, c, 0:128], wqcol[:]), 1)

            @blk.tensor
            def _(t_):
                t_.wait_ge(s_cl.h, 128)
                for t in range(TILES):
                    t_.wait_ge(s_xc.h, t + 1)
                    for q in range(4):
                        pq = pE if q % 2 == 0 else pF
                        t_.wait_ge(s_xev.h, max(0, 4 * t + q - 1))
                        last = None
                        for cc in range(8):
                            c = 8 * q + cc
                            last = t_.matmul(pq[:, cc * 128:(cc + 1) * 128],
                                             xrows[t % 2][:, c * 128:(c + 1) * 128],
                                             id128[:], start=True, stop=True)
                        s_xtr.inc(last)

            @blk.scalar
            def _(s):
                for t in range(TILES):
                    for q in range(4):
                        pq = pE if q % 2 == 0 else pF
                        s.wait_ge(s_xtr.h, 4 * t + q + 1)
                        dst = xT3[:, 8 * q:8 * q + 8, t * 128:(t + 1) * 128]
                        s_xev.inc(s.activation(dst, pq[:].rearrange("p (c n) -> p c n", c=8),
                                               AF.Copy))

            barrier(blk, {
                "gpsimd": [(ds_wq.sems[0], 16 * 16), (ds_wq.sems[1], 16 * 16), (s_cl2, 64), (s_bx2, 16)],
                "sync": [(s_cl, 128), (s_bx1, 16), (ds_x.sems[0], 16 * 8), (ds_x.sems[1], 16 * 8)],
                "vector": [(s_wqr, 32), (s_xc, TILES)],
                "scalar": [(s_xev, 64)],
                "tensor": [(s_xtr, 64)],
            })
        p0.close()

        pa = ExitStack(); st.enter_context(pa)
        repl = pa.enter_context(nc.sbuf_tensor("repl", [128, D], BF16))
        qf = pa.enter_context(nc.sbuf_tensor("qf", [128, 8192], BF16))
        vi = [pa.enter_context(nc.sbuf_tensor(f"vi{i}", [128, 256], BF16)) for i in range(2)]
        pbuf = [pa.enter_context(nc.sbuf_tensor(f"pbuf{i}", [128, 1024], BF16)) for i in range(2)]
        qkr = [pa.enter_context(nc.sbuf_tensor(f"qkr{i}", [128, 256], BF16)) for i in range(2)]
        vr = [pa.enter_context(nc.sbuf_tensor(f"vr{i}", [128, 128], BF16)) for i in range(2)]
        otb = [pa.enter_context(nc.sbuf_tensor(f"otb{i}", [128, 128], BF16)) for i in range(2)]
        rbuf = pa.enter_context(nc.sbuf_tensor("rbuf", [128, 128], F32))
        pre1 = pa.enter_context(nc.sbuf_tensor("pre1", [128, D], BF16))
        x1r = pa.enter_context(nc.sbuf_tensor("x1r", [128, D], BF16))
        bn6 = pa.enter_context(nc.sbuf_tensor("bn6", [128, 48], F32))
        mv = pa.enter_context(nc.sbuf_tensor("mv", [128, 2], F32))
        sdc = pa.enter_context(nc.sbuf_tensor("sdc", [128, 1], F32))
        rstd = pa.enter_context(nc.sbuf_tensor("rstd", [128, 1], F32))
        nmr = pa.enter_context(nc.sbuf_tensor("nmr", [128, 1], F32))

        with nc.Block() as blk:
            @blk.gpsimd
            def _(g):
                s_cl2.inc(g.dma_start(repl[:], repl_h.ap()), 16)
                g.memset(vi[0][:].rearrange("p (j two) -> p j two", two=2)[:, :, 1:2], 1.0)
                s_vi.inc(g.memset(
                    vi[1][:].rearrange("p (j two) -> p j two", two=2)[:, :, 1:2], 1.0))

            @blk.sync
            def _(sy):
                if dbg:
                    sy.wait_ge(s_wqr.h, 32)
                    sy.wait_ge(s_cl2.h, 64)
                    sy.wait_ge(s_bx.h, 33)
                    s_dbg.inc(sy.dma_start(dbg_wq, wqkv[:]), 16)
                    s_dbg.inc(sy.dma_start(dbg_br, brow[:]), 16)
                for t in range(TILES):
                    par = t % 2
                    sy.wait_ge(s_qke.h, 3 * t + 3)
                    if dbg:
                        s_dbg.inc(sy.dma_start(dbg_qk[t], qkr[par][:]), 16)
                    ds_fo.issue(sy, scr_qk[t], qkr[par][:])
                    for qq in range(4):
                        ds_fo.wait(sy, t)
                        sy.wait_ge(s_sg.h, max(0, 16 * t + 4 * qq))
                        rows = scr_qk[t][32 * qq:32 * qq + 32, :]
                        ds_fi.issue(
                            sy,
                            qf[0:1, 0:8192].rearrange("o (h p n) -> o h p n", h=2, p=32),
                            rows[:, 0:256].rearrange("p (h n) -> h p n", h=2).unsqueeze(0))
                    if dbg:
                        sy.wait_ge(s_dv.h, t + 1)
                        s_dbg.inc(sy.dma_start(dbg_o[t], otb[par][:]), 16)
                        sy.wait_ge(s_nm.h, t + 1)
                        s_dbg.inc(sy.dma_start(dbg_x1[t], x1r[:]), 16)

            @blk.tensor
            def _(t_):
                t_.wait_ge(s_cl.h, 128)
                t_.wait_ge(s_cl2.h, 80)
                t_.wait_ge(s_bx2.h, 16)
                t_.wait_ge(s_wqr.h, 32)

                def qkv_mm(t):
                    pq = pA if t % 2 == 0 else pB
                    if t >= 2:
                        t_.wait_ge(s_qke.h, 3 * (t - 2) + 3)
                    for c in range(NCH):
                        t_.matmul(pq[:, 0:384], xT3[:, c, t * 128:(t + 1) * 128],
                                  wq3[:, c, :], start=(c == 0), stop=False)
                    s_qkv.inc(t_.matmul(pq[:, 0:384], ones1[:], brow[:],
                                        start=False, stop=True))

                def pointwise(t):
                    par = t % 2
                    t_.wait_ge(s_qke.h, 3 * t + 3)
                    t_.wait_ge(s_vi.h, 1 + t)
                    s_vt.inc(t_.matmul(pC[:, par * 128:par * 128 + 128], vr[par][:],
                                       id128[:], start=True, stop=True))
                    if t >= 2:
                        t_.wait_ge(s_x1e.h, 4 * (t - 2))
                    if t >= 1:
                        t_.wait_ge(s_le.h, 4 * (t - 1))
                        t_.wait_ge(s_dv.h, t)
                    def sgrp(grp):
                        ps = pE if grp % 2 == 0 else pF
                        if grp % 4 == 0:
                            ds_fi.wait(t_, 4 * t + grp // 4)
                        t_.wait_ge(s_ex.h, max(0, 16 * t + grp - 1))
                        last = None
                        for j in range(8):
                            tok = (8 * grp + j) % 32
                            last = t_.matmul(ps[:, j * 128:(j + 1) * 128],
                                             qf[0:1, 4096 + tok * 128:4096 + (tok + 1) * 128],
                                             qf[0:1, tok * 128:(tok + 1) * 128],
                                             start=True, stop=True)
                        s_sg.inc(last)

                    def ndgrp(grp):
                        gp = grp % 2
                        t_.wait_ge(s_ex.h, 16 * t + grp + 1)
                        t_.wait_ge(s_vi.h, 2 + t)
                        last = None
                        for j in range(8):
                            tok = 8 * grp + j
                            off = par * 256 + 2 * tok
                            last = t_.matmul(pD[:, off:off + 2],
                                             pbuf[gp][:, j * 128:(j + 1) * 128],
                                             vi[par][:, 2 * tok:2 * tok + 2],
                                             start=True, stop=True)
                        s_nd.inc(last)

                    sgrp(0)
                    sgrp(1)
                    for grp in range(16):
                        ndgrp(grp)
                        if grp + 2 < 16:
                            sgrp(grp + 2)

                def ln1(t):
                    par = t % 2
                    t_.wait_ge(s_dv.h, t + 1)
                    t_.wait_ge(s_ex.h, 16 * min(t + 2, TILES))
                    if t >= 1:
                        t_.wait_ge(s_x1e.h, 4 * t)
                    for piece in range(4):
                        pp = pE if piece % 2 == 0 else pF
                        if piece >= 2:
                            t_.wait_ge(s_le.h, 4 * t + piece - 1)
                        last = None
                        for half in range(2):
                            col0 = piece * 1024 + half * 512
                            last = t_.matmul(pp[:, half * 512:half * 512 + 512],
                                             otb[par][:], repl[:, col0:col0 + 512],
                                             start=True, stop=False)
                            for cc in range(4 * half, 4 * half + 4):
                                c = piece * 8 + cc
                                last = t_.matmul(pp[:, cc * 128:(cc + 1) * 128],
                                                 xT3[:, c, t * 128:(t + 1) * 128], id128[:],
                                                 start=False, stop=(cc % 4 == 3))
                        s_ln.inc(last)

                def x1t(t, ln_guard=None):
                    t_.wait_ge(s_nm.h, t + 1)
                    if ln_guard is not None:
                        t_.wait_ge(s_le.h, 4 * ln_guard)
                    for q in range(4):
                        pq = pE if q % 2 == 0 else pF
                        t_.wait_ge(s_x1e.h, max(0, 4 * t + q - 1))
                        last = None
                        for cc in range(8):
                            c = 8 * q + cc
                            last = t_.matmul(pq[:, cc * 128:(cc + 1) * 128],
                                             x1r[:, c * 128:(c + 1) * 128], id128[:],
                                             start=True, stop=True)
                        s_x1.inc(last)

                qkv_mm(0)
                for t in range(TILES):
                    if t + 1 < TILES:
                        qkv_mm(t + 1)
                    pointwise(t)
                    if t >= 2:
                        x1t(t - 2)
                    if t >= 1:
                        ln1(t - 1)
                x1t(TILES - 2, ln_guard=TILES - 1)
                ln1(TILES - 1)
                x1t(TILES - 1, ln_guard=TILES)

            @blk.scalar
            def _(s):
                def act_qkv_exp(t):
                    par = t % 2
                    pq = pA if par == 0 else pB
                    s.wait_ge(s_qkv.h, t + 1)
                    if t >= 2:
                        ds_fo.wait(s, t - 2)
                        s.wait_ge(s_vt.h, t - 1)
                    s.activation(qkr[par][:, 0:128], pq[:, 0:128], AF.Copy, scale=SCALE_Q)
                    s.activation(qkr[par][:, 128:256], pq[:, 128:256], AF.Copy)
                    s_qke.inc(s.activation(vr[par][:], pq[:, 256:384], AF.Copy), 3)
                    for grp in range(16):
                        gp = grp % 2
                        ps = pE if gp == 0 else pF
                        s.wait_ge(s_sg.h, 16 * t + grp + 1)
                        s.wait_ge(s_nd.h, max(0, 16 * t + grp - 1))
                        s_ex.inc(s.activation(pbuf[gp][:], ps[:], AF.Exp))

                def act_ln(t):
                    for piece in range(4):
                        pp = pE if piece % 2 == 0 else pF
                        s.wait_ge(s_ln.h, 4 * t + piece + 1)
                        if t >= 1 and piece == 0:
                            s.wait_ge(s_stt.h, 3 * (t - 1) + 1)
                        s_le.inc(s.activation(pre1[:, piece * 1024:(piece + 1) * 1024],
                                              pp[:], AF.Copy))
                    s.wait_ge(s_stt.h, 3 * t + 1)
                    s_stt.inc(s.activation(sdc[:], mv[:, 1:2], AF.Sqrt, bias=epsc[:]))
                    s.wait_ge(s_stt.h, 3 * t + 3)
                    s.wait_ge(s_le.h, 4 * (t + 1))
                    if t >= 1:
                        s.wait_ge(s_x1.h, 4 * t)
                    s_nm.inc(s.activation(x1r[:], pre1[:], AF.Identity, bias=nmr[:],
                                          scale=rstd[:]))

                def act_x1e(t):
                    for q in range(4):
                        s.wait_ge(s_x1.h, 4 * t + q + 1)
                        pq2 = pE if q % 2 == 0 else pF
                        dst = xT3[:, 8 * q:8 * q + 8, t * 128:(t + 1) * 128]
                        s_x1e.inc(s.activation(dst, pq2[:].rearrange("p (c n) -> p c n", c=8),
                                               AF.Copy))

                for t in range(TILES):
                    act_qkv_exp(t)
                    if t >= 2:
                        act_x1e(t - 2)
                    if t >= 1:
                        act_ln(t - 1)
                act_x1e(TILES - 2)
                act_ln(TILES - 1)
                act_x1e(TILES - 1)

            @blk.vector
            def _(v):
                def dve_pw(t):
                    par = t % 2
                    v.wait_ge(s_vt.h, t + 1)
                    if t >= 2:
                        v.wait_ge(s_nd.h, 16 * (t - 1))
                    s_vi.inc(v.tensor_copy(
                        vi[par][:].rearrange("p (j two) -> p j two", two=2)[:, :, 0:1],
                        pC[:, par * 128:par * 128 + 128].unsqueeze(2)))
                    v.wait_ge(s_nd.h, 16 * (t + 1))
                    nd3 = pD[:, par * 256:par * 256 + 256].rearrange(
                        "p (j two) -> p j two", two=2)
                    vst(v, v.reciprocal(rbuf[:], nd3[:, :, 1:2].squeeze(2)))
                    vwt(v)
                    s_dv.inc(v.tensor_tensor(otb[par][:], nd3[:, :, 0:1].squeeze(2),
                                             rbuf[:], ALU.mult))

                def dve_stats(t):
                    v.wait_ge(s_le.h, 4 * (t + 1))
                    if t >= 1:
                        v.wait_ge(s_nm.h, t)
                    for i in range(8):
                        vst(v, v.bn_stats(bn6[:, 6 * i:6 * i + 6],
                                          pre1[:, 512 * i:512 * i + 512]))
                    vwt(v)
                    s_stt.inc(v.bn_aggr(mv[:], bn6[:]))
                    v.wait_ge(s_stt.h, 3 * t + 2)
                    vst(v, v.reciprocal(rstd[:], sdc[:]))
                    vwt(v)
                    s_stt.inc(v.scalar_tensor_tensor(nmr[:], mv[:, 0:1], -1.0, rstd[:],
                                                     ALU.mult, ALU.mult))

                for t in range(TILES):
                    dve_pw(t)
                    if t >= 1:
                        dve_stats(t - 1)
                dve_stats(TILES - 1)

            barrier(blk, {
                "gpsimd": [(s_cl2, 80), (s_vi, 1)],
                "sync": [(ds_fo.sems[0], 16 * 8), (ds_fo.sems[1], 16 * 8), (ds_fi.sems[0], 16 * 32), (ds_fi.sems[1], 16 * 32)],
                "vector": [(s_stt, 3 * TILES)],
                "scalar": [(s_x1e, 4 * TILES)],
                "tensor": [(s_x1, 4 * TILES)],
            })
        pa.close()
        p01.close()

        fc = ExitStack(); st.enter_context(fc)
        slab = [fc.enter_context(nc.sbuf_tensor(f"slab{i}", [128, NCH * SLABW], BF16))
                for i in range(2)]
        btb = fc.enter_context(nc.sbuf_tensor("btb", [128, D], BF16))
        bbr = [fc.enter_context(nc.sbuf_tensor(f"bbr{i}", [1, SLABW], BF16)) for i in range(2)]

        def load_slab(g, si):
            par = si % 2
            sl3 = slab[par][:].rearrange("p (c n) -> p c n", c=NCH)
            for dd in range(4):
                src = wt_d[dd * 1024:(dd + 1) * 1024, si * SLABW:(si + 1) * SLABW]
                ds_sl.issue(g, sl3[:, 8 * dd:8 * dd + 8, :],
                            src.rearrange("(c p) n -> p c n", p=128))
            ds_bb.issue(g, bbr[par][:],
                        bb_d[si * SLABW:(si + 1) * SLABW].unsqueeze(0))

        b0 = ExitStack(); st.enter_context(b0)
        hT = b0.enter_context(nc.sbuf_tensor("hT", [128, NT], BF16))
        atb = b0.enter_context(nc.sbuf_tensor("atb", [128, NCH * 128], BF16))
        rwb = b0.enter_context(nc.sbuf_tensor("rwb", [128, NCH * 16], BF16))
        lgb = b0.enter_context(nc.sbuf_tensor("lgb", [15, NT], BF16))
        erow = b0.enter_context(nc.sbuf_tensor("erow", [128, 15], F32))
        prob = b0.enter_context(nc.sbuf_tensor("prob", [128, 15], F32))
        rank = b0.enter_context(nc.sbuf_tensor("rank", [128, 15], F32))
        selm = b0.enter_context(nc.sbuf_tensor("selm", [128, 15], F32))
        wcb = b0.enter_context(nc.sbuf_tensor("wcb", [128, 15], BF16))
        wctb = b0.enter_context(nc.sbuf_tensor("wctb", [15, 128], BF16))
        w8b = b0.enter_context(nc.sbuf_tensor("w8b", [8, 128], BF16))
        sj = b0.enter_context(nc.sbuf_tensor("sj", [128, 16 * 15], BF16))
        nmx = b0.enter_context(nc.sbuf_tensor("nmx", [128, 1], F32))
        sme = b0.enter_context(nc.sbuf_tensor("sme", [128, 1], F32))
        rcp = b0.enter_context(nc.sbuf_tensor("rcp", [128, 1], F32))
        a3 = atb[:].rearrange("p (c n) -> p c n", c=NCH)
        r3 = rwb[:].rearrange("p (c n) -> p c n", c=NCH)

        with nc.Block() as blk:
            @blk.gpsimd
            def _(g):
                s_b0d.inc(g.dma_start(a3, at_d.rearrange("(c p) n -> p c n", p=128)), 16)
                s_b0d.inc(g.dma_start(r3[:, :, 0:15],
                                      rwt_d.rearrange("(c p) n -> p c n", p=128)), 16)
                s_cl2.inc(g.dma_start(btb[:], bt_d), 16)
                load_slab(g, 0)
                load_slab(g, 1)

            @blk.tensor
            def _(t_):
                t_.wait_ge(s_b0d.h, 32)
                for q in range(4):
                    pq = pE if q < 2 else pF
                    cols = (q % 2) * 512
                    last = None
                    for c in range(NCH):
                        last = t_.matmul(pq[:, cols:cols + 512], a3[:, c, :],
                                         xT3[:, c, q * 512:(q + 1) * 512],
                                         start=(c == 0), stop=(c == NCH - 1))
                s_b0m.inc(last)
                for q in range(4):
                    pq = (pA, pB, pC, pD)[q]
                    last = None
                    for c in range(NCH):
                        last = t_.matmul(pq[0:15, 0:512], r3[:, c, 0:15],
                                         xT3[:, c, q * 512:(q + 1) * 512],
                                         start=(c == 0), stop=(c == NCH - 1))
                s_b0m.inc(last)
                for t in range(TILES):
                    t_.wait_ge(s_b0e.h, 3)
                    t_.wait_ge(s_pr.h, 1)
                    if t >= 1:
                        t_.wait_ge(s_wce.h, t)
                    s_lgt.inc(t_.matmul(pC[:, 256:271], lgb[:, t * 128:(t + 1) * 128],
                                        id15[:], start=True, stop=True))
                    t_.wait_ge(s_pr.h, 1 + 2 * t + 2)
                    s_wct.inc(t_.matmul(pC[0:15, 272:400], wcb[:], id128[:],
                                        start=True, stop=True))
                    t_.wait_ge(s_wce.h, t + 1)
                    if t >= 1:
                        t_.wait_ge(s_hw.h, t)
                    s_w8.inc(t_.matmul(pD[0:8, 256:384], ohb[:], wctb[:],
                                       start=True, stop=True))
                    t_.wait_ge(s_w8e.h, t + 1)
                    s_w128.inc(t_.matmul(pD[:, 384:512], r16[:], w8b[:],
                                         start=True, stop=True))

            @blk.scalar
            def _(s):
                s.wait_ge(s_b0m.h, 1)
                s.activation(hT[:, 0:512], pE[:, 0:512], AF.Copy)
                s.activation(hT[:, 512:1024], pE[:, 512:1024], AF.Copy)
                s.activation(hT[:, 1024:1536], pF[:, 0:512], AF.Copy)
                s_b0e.inc(s.activation(hT[:, 1536:2048], pF[:, 512:1024], AF.Copy))
                s.wait_ge(s_b0m.h, 2)
                for q in range(4):
                    pq = (pA, pB, pC, pD)[q]
                    ins = s.activation(lgb[:, q * 512:(q + 1) * 512], pq[0:15, 0:512],
                                       AF.Identity, bias=rbc[:])
                s_b0e.inc(ins, 2)
                for t in range(TILES):
                    s.wait_ge(s_pr.h, 1 + 2 * t + 1)
                    if t >= 1:
                        s.wait_ge(s_pr.h, 1 + 2 * (t - 1) + 2)
                    s_exr.inc(s.activation(erow[:], pC[:, 256:271], AF.Exp, bias=nmx[:]))
                    s.wait_ge(s_wct.h, t + 1)
                    s_wce.inc(s.activation(wctb[:], pC[0:15, 272:400], AF.Copy))
                    s.wait_ge(s_w8.h, t + 1)
                    s_w8e.inc(s.activation(w8b[:], pD[0:8, 256:384], AF.Copy, scale=2.0))

            @blk.vector
            def _(v):
                v.wait_ge(s_cl.h, 128)
                vst(v, v.tensor_copy(emf[:], emc[:]))
                vwt(v)
                vst(v, v.tensor_scalar(out=selm[0:15, 0:8], in0=iota[:], scalar1=emf[:],
                                       scalar2=0.0, op0=ALU.is_equal, op1=ALU.add))
                vwt(v)
                s_pr.inc(v.tensor_copy(ohb[:], selm[0:15, 0:8]))
                for t in range(TILES):
                    v.wait_ge(s_lgt.h, t + 1)
                    if t >= 1:
                        v.wait_ge(s_exr.h, t)
                    s_pr.inc(v.reduce_max(nmx[:], pC[:, 256:271], axis=AX.X,
                                          negate=True))
                    v.wait_ge(s_exr.h, t + 1)
                    vst(v, v.reduce_sum(sme[:], erow[:], axis=AX.X))
                    vwt(v)
                    vst(v, v.reciprocal(rcp[:], sme[:]))
                    vwt(v)
                    vst(v, v.tensor_scalar(out=prob[:], in0=erow[:], scalar1=rcp[:],
                                           scalar2=None, op0=ALU.mult))
                    vwt(v)
                    for j in range(15):
                        ins = v.tensor_scalar(out=sj[:, 15 * j:15 * j + 15], in0=prob[:],
                                              scalar1=prob[:, j:j + 1],
                                              scalar2=0.0, op0=ALU.is_gt, op1=ALU.add,
                                              accum_out=rank[:, j:j + 1])
                        if j == 14:
                            vst(v, ins)
                    vwt(v)
                    vst(v, v.tensor_scalar(out=selm[:], in0=rank[:], scalar1=7.5,
                                           scalar2=None, op0=ALU.is_lt))
                    vwt(v)
                    s_pr.inc(v.tensor_tensor(wcb[:], prob[:], selm[:], ALU.mult))
                    v.wait_ge(s_w128.h, t + 1)
                    s_hw.inc(v.tensor_tensor(hwT[:, t * 128:(t + 1) * 128],
                                             hT[:, t * 128:(t + 1) * 128],
                                             pD[:, 384:512], ALU.mult))

            @blk.sync
            def _(sy):
                if dbg:
                    for t in range(TILES):
                        sy.wait_ge(s_pr.h, 1 + 2 * t + 2)
                        s_dbg.inc(sy.dma_start(dbg_pr[t], prob[:]), 16)
                        sy.wait_ge(s_w8e.h, t + 1)
                        s_dbg.inc(sy.dma_start(dbg_w[t], w8b[:]), 16)
                    sy.wait_ge(s_hw.h, 16)
                    s_dbg.inc(sy.dma_start(dbg_hw, hwT[:]), 16)
                    sy.wait_ge(s_dbg.h, s_dbg.n)

            barrier(blk, {
                "gpsimd": [(s_b0d, 32)],
                "sync": [],
                "vector": [(s_hw, TILES)],
                "scalar": [(s_w8e, TILES)],
                "tensor": [(s_w128, TILES)],
            })
        b0.close()

        fc2 = ExitStack(); st.enter_context(fc2)
        outb = [fc2.enter_context(nc.sbuf_tensor(f"outb{i}", [128, SLABW], BF16))
                for i in range(2)]
        bnf = fc2.enter_context(nc.sbuf_tensor("bnf", [128, TILES * 96], F32))
        pre2 = [fc2.enter_context(nc.sbuf_tensor(f"pre2_{i}", [128, D], BF16))
                for i in range(2)]
        our = fc2.enter_context(nc.sbuf_tensor("our", [128, 2048], F32))
        mv2 = fc2.enter_context(nc.sbuf_tensor("mv2", [128, 2], F32))
        sd2 = fc2.enter_context(nc.sbuf_tensor("sd2", [128, 1], F32))
        rs2 = fc2.enter_context(nc.sbuf_tensor("rs2", [128, 1], F32))
        nm2 = fc2.enter_context(nc.sbuf_tensor("nm2", [128, 1], F32))

        with nc.Block() as blk:
            @blk.gpsimd
            def _(g):
                for si in range(2, NSLAB):
                    g.wait_ge(s_fm.h, TILES * (si - 1))
                    load_slab(g, si)

            @blk.tensor
            def _(t_):
                t_.wait_ge(s_hw.h, TILES)
                t_.wait_ge(s_cl2.h, 96)
                for si in range(NSLAB):
                    par = si % 2
                    ds_sl.wait_all_upto(t_, 4 * si + 3)
                    ds_bb.wait(t_, si)
                    sl3 = slab[par][:].rearrange("p (c n) -> p c n", c=NCH)
                    for t in range(TILES):
                        k = si * TILES + t
                        pq = pA if k % 2 == 0 else pB
                        if k >= 2:
                            t_.wait_ge(s_fe.h, k - 1)
                        for c in range(NCH):
                            t_.matmul(pq[:, 0:SLABW], xT3[:, c, t * 128:(t + 1) * 128],
                                      sl3[:, c, :], start=(c == 0), stop=False)
                        for cc in range(2):
                            c = 2 * si + cc
                            t_.matmul(pq[:, cc * 128:(cc + 1) * 128],
                                      xT3[:, c, t * 128:(t + 1) * 128], id128[:],
                                      start=False, stop=False)
                        t_.matmul(pq[:, 0:SLABW], hwT[:, t * 128:(t + 1) * 128],
                                  btb[:, si * SLABW:(si + 1) * SLABW],
                                  start=False, stop=False)
                        s_fm.inc(t_.matmul(pq[:, 0:SLABW], ones1[:], bbr[par][:],
                                           start=False, stop=True))

            @blk.scalar
            def _(s):
                for si in range(NSLAB):
                    for t in range(TILES):
                        k = si * TILES + t
                        pq = pA if k % 2 == 0 else pB
                        s.wait_ge(s_fm.h, k + 1)
                        if k >= 2:
                            ds_sp.wait(s, k - 2)
                            s.wait_ge(s_fb.h, k - 1)
                        s_fe.inc(s.activation(outb[k % 2][:], pq[:, 0:SLABW], AF.Copy))
                        if si == NSLAB - 1:
                            s.wait_ge(s_pcs.h, 3 * t + 1)
                            s_pcs.inc(s.activation(sd2[:], mv2[:, 1:2], AF.Sqrt,
                                                   bias=epsc[:]))
                            s.wait_ge(s_pcs.h, 3 * t + 3)
                            ds_p2.wait(s, t)
                            for half in range(2):
                                if 2 * t + half >= 1:
                                    ds_y.wait(s, 2 * t + half - 1)
                                s_pcn.inc(s.activation(
                                    our[:], pre2[t % 2][:, half * 2048:(half + 1) * 2048],
                                    AF.Identity, bias=nm2[:], scale=rs2[:]))

            @blk.vector
            def _(v):
                for si in range(NSLAB):
                    for t in range(TILES):
                        k = si * TILES + t
                        v.wait_ge(s_fe.h, k + 1)
                        s_fb.inc(v.bn_stats(bnf[:, t * 96 + si * 6:t * 96 + si * 6 + 6],
                                            outb[k % 2][:]))
                        if si == NSLAB - 1:
                            v.wait_ge(s_fb.h, k + 1)
                            if t >= 1:
                                v.wait_ge(s_pcn.h, 2 * t)
                            s_pcs.inc(v.bn_aggr(mv2[:], bnf[:, t * 96:t * 96 + 96]))
                            v.wait_ge(s_pcs.h, 3 * t + 2)
                            vst(v, v.reciprocal(rs2[:], sd2[:]))
                            vwt(v)
                            s_pcs.inc(v.scalar_tensor_tensor(nm2[:], mv2[:, 0:1], -1.0,
                                                             rs2[:], ALU.mult, ALU.mult))

            @blk.sync
            def _(sy):
                for si in range(NSLAB):
                    for t in range(TILES):
                        k = si * TILES + t
                        sy.wait_ge(s_fe.h, k + 1)
                        ds_sp.issue(sy,
                                    pre2_d[t * 128:(t + 1) * 128,
                                           si * SLABW:(si + 1) * SLABW],
                                    outb[k % 2][:])
                        if si == NSLAB - 1:
                            ds_sp.wait_all_upto(sy, (NSLAB - 1) * TILES + t)
                            if t >= 2:
                                sy.wait_ge(s_pcn.h, 2 * (t - 2) + 2)
                            ds_p2.issue(sy, pre2[t % 2][:],
                                        pre2_d[t * 128:(t + 1) * 128, :])
                            for half in range(2):
                                sy.wait_ge(s_pcn.h, 2 * t + half + 1)
                                ds_y.issue(sy,
                                           y_d[t * 128:(t + 1) * 128,
                                               half * 2048:(half + 1) * 2048],
                                           our[:])
                ds_y.wait_done(sy)

            barrier(blk, {
                "gpsimd": [(ds_sl.sems[0], 16 * 32), (ds_sl.sems[1], 16 * 32),
                           (ds_bb.sems[0], 16 * 8), (ds_bb.sems[1], 16 * 8)],
                "sync": [],
                "vector": [(s_pcs, 3 * TILES)],
                "scalar": [(s_pcn, 2 * TILES)],
                "tensor": [(s_fm, NSLAB * TILES)],
            })
        fc2.close()
        fc.close()

        assert s_qkv.n == TILES and s_qke.n == 3 * TILES, (s_qkv.n, s_qke.n)
        assert s_sg.n == 16 * TILES and s_ex.n == 16 * TILES and s_nd.n == 16 * TILES
        assert s_dv.n == TILES
        assert s_ln.n == 4 * TILES and s_le.n == 4 * TILES
        assert s_x1.n == 4 * TILES and s_x1e.n == 4 * TILES
        assert s_stt.n == 3 * TILES and s_nm.n == TILES
        assert s_fm.n == NSLAB * TILES and s_fe.n == NSLAB * TILES
        assert s_fb.n == NSLAB * TILES
        assert s_pcs.n == 3 * TILES and s_pcn.n == 2 * TILES
        assert ds_y.i == 2 * TILES and ds_sp.i == NSLAB * TILES

    return nc


def _stage(inputs):
    f = np.ascontiguousarray
    x = np.asarray(inputs["L0_attn_x"], np.float32).reshape(16384, D)
    shared = dict(
        wq=np.asarray(inputs["L0_attn_Wq"], np.float32).reshape(D, D),
        wk=np.asarray(inputs["L0_attn_Wk"], np.float32).reshape(D, 128),
        wv=np.asarray(inputs["L0_attn_Wv"], np.float32).reshape(D, 128),
        bq=np.asarray(inputs["L0_attn_bq"], np.float32),
        bk=np.asarray(inputs["L0_attn_bk"], np.float32).reshape(128),
        bv=np.asarray(inputs["L0_attn_bv"], np.float32).reshape(128),
        wt=f(np.asarray(inputs["L0_ffn_base_W"], np.float32).T),
        bb=np.asarray(inputs["L0_ffn_base_b"], np.float32),
        rwt=f(np.asarray(inputs["L0_ffn_router_W"], np.float32).T),
        rb=np.asarray(inputs["L0_ffn_router_b"], np.float32),
        at=f(np.asarray(inputs["L0_ffn_A"], np.float32).reshape(128, D).T),
        bt=f(np.asarray(inputs["L0_ffn_B"], np.float32).transpose(0, 2, 1).reshape(128, D)),
        emap=np.asarray(inputs["L0_ffn_expert_map"], np.int32),
    )
    return x, shared


def kernel(**inputs):
    x, shared = _stage(inputs)
    nc = build_program()
    in_maps = [dict(shared, x=np.ascontiguousarray(x[c * NT:(c + 1) * NT]))
               for c in range(8)]
    res = run_bass_kernel_spmd(nc, in_maps, core_ids=list(range(8)))
    out = np.concatenate([np.asarray(res.results[c]["y"]) for c in range(8)], axis=0)
    return out.reshape(2, 8192, D).astype(np.float32)


# version 09
# speedup vs baseline: 1.2690x; 1.2690x over previous
from contextlib import ExitStack

import numpy as np
import ml_dtypes
import concourse.bass as bass
import concourse.mybir as mybir
from concourse.bass_utils import run_bass_kernel_spmd

F32 = mybir.dt.float32
BF16 = mybir.dt.bfloat16
I32 = mybir.dt.int32
AF = mybir.ActivationFunctionType
ALU = mybir.AluOpType
AX = mybir.AxisListType

NT = 2048
TILES = 16
NCH = 32
D = 4096
SCALE_Q = 1.0 / float(np.sqrt(128.0))
NSLAB = 16
SLABW = 256
EPS = 1e-5


class Sem:
    def __init__(self, nc, st, name):
        self.h = st.enter_context(nc.semaphore(name))
        self.n = 0

    def inc(self, instr, amount=1):
        instr.then_inc(self.h, amount)
        self.n += amount
        return self.n


class DStream:

    def __init__(self, nc, st, name, r=2):
        self.sems = [Sem(nc, st, f"{name}{i}") for i in range(r)]
        self.r = r
        self.i = 0

    def issue(self, eng, out, in_, **kw):
        p = self.i % self.r
        if self.i >= self.r:
            eng.wait_ge(self.sems[p].h, 16 * (self.i // self.r))
        self.sems[p].inc(eng.dma_start(out, in_, **kw), 16)
        self.i += 1
        return self.i - 1

    def wait(self, eng, idx):
        p = idx % self.r
        eng.wait_ge(self.sems[p].h, 16 * (idx // self.r + 1))

    def wait_all_upto(self, eng, idx):
        for p in range(self.r):
            last = idx - ((idx - p) % self.r)
            if last >= 0:
                eng.wait_ge(self.sems[p].h, 16 * (last // self.r + 1))

    def wait_done(self, eng):
        for p in range(self.r):
            last = self.i - 1 - ((self.i - 1 - p) % self.r)
            if last >= 0:
                eng.wait_ge(self.sems[p].h, 16 * (last // self.r + 1))


def build_program(dbg=False):
    nc = bass.Bass("TRN2", target_bir_lowering=False)

    x_d = nc.dram_tensor("x", [NT, D], F32, kind="ExternalInput").ap()
    wq_d = nc.dram_tensor("wq", [D, D], F32, kind="ExternalInput").ap()
    wk_d = nc.dram_tensor("wk", [D, 128], F32, kind="ExternalInput").ap()
    wv_d = nc.dram_tensor("wv", [D, 128], F32, kind="ExternalInput").ap()
    bq_d = nc.dram_tensor("bq", [128, 32], F32, kind="ExternalInput").ap()
    bk_d = nc.dram_tensor("bk", [128], F32, kind="ExternalInput").ap()
    bv_d = nc.dram_tensor("bv", [128], F32, kind="ExternalInput").ap()
    wt_d = nc.dram_tensor("wt", [D, D], F32, kind="ExternalInput").ap()
    bb_d = nc.dram_tensor("bb", [D], F32, kind="ExternalInput").ap()
    rwt_d = nc.dram_tensor("rwt", [D, 15], F32, kind="ExternalInput").ap()
    rb_d = nc.dram_tensor("rb", [15], F32, kind="ExternalInput").ap()
    at_d = nc.dram_tensor("at", [D, 128], F32, kind="ExternalInput").ap()
    bt_d = nc.dram_tensor("bt", [128, D], F32, kind="ExternalInput").ap()
    em_d = nc.dram_tensor("emap", [15], I32, kind="ExternalInput").ap()
    y_d = nc.dram_tensor("y", [NT, D], F32, kind="ExternalOutput").ap()

    scr_qk = nc.dram_tensor("scr_qk", [TILES, 128, 256], BF16, kind="Internal").ap()
    scr_b = nc.dram_tensor("scr_b", [128], F32, kind="Internal").ap()
    pre2_d = nc.dram_tensor("pre2", [NT, D], BF16, kind="Internal").ap()

    if dbg:
        dbg_o = nc.dram_tensor("dbg_o", [TILES, 128, 128], BF16, kind="ExternalOutput").ap()
        dbg_x1 = nc.dram_tensor("dbg_x1", [TILES, 128, D], BF16, kind="ExternalOutput").ap()
        dbg_pr = nc.dram_tensor("dbg_pr", [TILES, 128, 15], F32, kind="ExternalOutput").ap()
        dbg_w = nc.dram_tensor("dbg_w", [TILES, 8, 128], BF16, kind="ExternalOutput").ap()
        dbg_hw = nc.dram_tensor("dbg_hw", [128, NT], BF16, kind="ExternalOutput").ap()
        dbg_wq = nc.dram_tensor("dbg_wq", [128, NCH * 384], BF16, kind="ExternalOutput").ap()
        dbg_br = nc.dram_tensor("dbg_br", [1, 384], BF16, kind="ExternalOutput").ap()
        dbg_qk = nc.dram_tensor("dbg_qk", [TILES, 128, 256], BF16, kind="ExternalOutput").ap()

    id128_h = nc.inline_tensor(np.eye(128, dtype=np.float32).astype(ml_dtypes.bfloat16), "c_id128")
    id15_h = nc.inline_tensor(np.eye(15, dtype=np.float32).astype(ml_dtypes.bfloat16), "c_id15")
    repl_np = np.zeros((128, D), np.float32)
    repl_np[np.arange(D) // 32, np.arange(D)] = 1.0
    repl_h = nc.inline_tensor(repl_np.astype(ml_dtypes.bfloat16), "c_repl")
    r16_np = np.zeros((8, 128), np.float32)
    r16_np[np.arange(128) // 16, np.arange(128)] = 1.0
    r16_h = nc.inline_tensor(r16_np.astype(ml_dtypes.bfloat16), "c_r16")
    iota_h = nc.inline_tensor(np.tile(np.arange(8, dtype=np.float32), (15, 1)), "c_iota")
    ones_h = nc.inline_tensor(np.ones((1, 128), np.float32).astype(ml_dtypes.bfloat16), "c_ones")

    st = ExitStack()
    with st:
        E = st.enter_context

        xT = E(nc.sbuf_tensor("xT", [128, NCH * NT], BF16))
        id128 = E(nc.sbuf_tensor("id128", [128, 128], BF16))
        id15 = E(nc.sbuf_tensor("id15", [15, 15], BF16))
        ones1 = E(nc.sbuf_tensor("ones1", [1, 128], BF16))
        brow = E(nc.sbuf_tensor("brow", [1, 384], BF16))
        r16 = E(nc.sbuf_tensor("r16", [8, 128], BF16))
        iota = E(nc.sbuf_tensor("iota", [15, 8], F32))
        emc = E(nc.sbuf_tensor("emc", [15, 1], I32))
        emf = E(nc.sbuf_tensor("emf", [15, 1], F32))
        ohb = E(nc.sbuf_tensor("ohb", [15, 8], BF16))
        rbc = E(nc.sbuf_tensor("rbc", [15, 1], F32))
        epsc = E(nc.sbuf_tensor("epsc", [128, 1], F32))
        hwT = E(nc.sbuf_tensor("hwT", [128, NT], BF16))

        pA = E(nc.psum_tensor("pA", [128, 512], F32))
        pB = E(nc.psum_tensor("pB", [128, 512], F32))
        pC = E(nc.psum_tensor("pC", [128, 512], F32))
        pD = E(nc.psum_tensor("pD", [128, 512], F32))
        pE = E(nc.psum_tensor("pE", [128, 1024], F32))
        pF = E(nc.psum_tensor("pF", [128, 1024], F32))

        sst = ExitStack()
        st.enter_context(sst)
        mk = lambda name: Sem(nc, sst, name)
        s_cl = mk("cl"); s_xl = mk("xl"); s_xtr = mk("xtr"); s_xev = mk("xev")
        s_wql = mk("wql"); s_wqr = mk("wqr"); s_bx = mk("bx")
        s_qkv = mk("qkv"); s_qke = mk("qke"); s_fl = mk("fl"); s_vt = mk("vt")
        s_vi = mk("vi"); s_sg = mk("sg"); s_ex = mk("ex"); s_nd = mk("nd")
        s_dv = mk("dv"); s_ln = mk("ln"); s_le = mk("le"); s_stt = mk("stt")
        s_nm = mk("nm"); s_x1 = mk("x1"); s_x1e = mk("x1e")
        s_b0m = mk("b0m"); s_b0e = mk("b0e"); s_lgt = mk("lgt"); s_pr = mk("pr")
        s_exr = mk("exr"); s_wct = mk("wct"); s_wce = mk("wce"); s_w8 = mk("w8")
        s_w8e = mk("w8e"); s_w128 = mk("w128"); s_hw = mk("hw")
        s_wl = mk("wl"); s_bl = mk("bl"); s_fm = mk("fm"); s_fe = mk("fe")
        s_fb = mk("fb"); s_sp = mk("sp"); s_p2 = mk("p2"); s_pcs = mk("pcs")
        s_pcn = mk("pcn"); s_out = mk("out"); s_ph = mk("ph"); s_cl2 = mk("cl2"); s_dbg = mk("dbg")
        s_vv = mk("vv"); s_bx1 = mk("bx1"); s_bx2 = mk("bx2"); s_b0d = mk("b0d"); s_xc = mk("xc")
        ds_x = DStream(nc, sst, "dsx", 3)
        ds_wq = DStream(nc, sst, "dswq", 3)
        ds_fo = DStream(nc, sst, "dsfo")
        ds_fi = DStream(nc, sst, "dsfi")
        ds_sl = DStream(nc, sst, "dssl")
        ds_bb = DStream(nc, sst, "dsbb")
        ds_sp = DStream(nc, sst, "dssp")
        ds_p2 = DStream(nc, sst, "dsp2")
        ds_y = DStream(nc, sst, "dsy")

        def vst(v, ins):
            s_vv.inc(ins)

        def vwt(v):
            v.wait_ge(s_vv.h, s_vv.n)

        xT3 = xT[:].rearrange("p (c n) -> p c n", c=NCH)

        PH = [0]

        def barrier(blk, guards):
            PH[0] += 1
            tgt = 5 * PH[0]
            for name in ("gpsimd", "sync", "vector", "scalar", "tensor"):
                def _f(eng, _g=guards.get(name, ()), _t=tgt):
                    for sem, val in _g:
                        eng.wait_ge(sem.h, val)
                    eng.nop().then_inc(s_ph.h, 1)
                    eng.wait_ge(s_ph.h, _t)
                getattr(blk, name)(_f)
            s_ph.n = tgt

        p01 = ExitStack(); st.enter_context(p01)
        wqkv = p01.enter_context(nc.sbuf_tensor("wqkv", [128, NCH * 384], BF16))
        p0 = ExitStack(); st.enter_context(p0)
        xrows = [p0.enter_context(nc.sbuf_tensor(f"xrows{i}", [128, D], BF16)) for i in range(3)]
        wqst = [p0.enter_context(nc.sbuf_tensor(f"wqst{i}", [128, D], BF16)) for i in range(3)]
        wq3 = wqkv[:].rearrange("p (c n) -> p c n", c=NCH)
        wqcol = p0.enter_context(nc.sbuf_tensor("wqcol", [128, 128], F32))
        bqs = p0.enter_context(nc.sbuf_tensor("bqs", [128, 32], F32))
        bqc = p0.enter_context(nc.sbuf_tensor("bqc", [128, 1], F32))

        with nc.Block() as blk:
            @blk.sync
            def _(sy):
                s_cl.inc(sy.dma_start(id128[:], id128_h.ap()), 16)
                s_cl.inc(sy.dma_start(id15[:], id15_h.ap()), 16)
                s_cl.inc(sy.dma_start(ones1[:], ones_h.ap()), 16)
                s_cl.inc(sy.dma_start(r16[:], r16_h.ap()), 16)
                s_cl.inc(sy.dma_start(iota[:], iota_h.ap()), 16)
                s_cl.inc(sy.dma_start(emc[:], em_d.unsqueeze(1)), 16)
                s_cl.inc(sy.dma_start(rbc[:], rb_d.unsqueeze(1)), 16)
                s_cl.inc(sy.dma_start(bqs[:], bq_d), 16)
                sy.wait_ge(s_bx.h, 1)
                s_bx1.inc(sy.dma_start(scr_b.unsqueeze(1), bqc[:]), 16)

            @blk.gpsimd
            def _(g):
                s_cl2.inc(g.dma_start(wq3[:, :, 128:256],
                                      wk_d.rearrange("(c p) n -> p c n", p=128)), 16)
                s_cl2.inc(g.dma_start(wq3[:, :, 256:384],
                                      wv_d.rearrange("(c p) n -> p c n", p=128)), 16)
                s_cl2.inc(g.dma_start(brow[0:1, 128:256], bk_d.unsqueeze(0)), 16)
                s_cl2.inc(g.dma_start(brow[0:1, 256:384], bv_d.unsqueeze(0)), 16)
                for t in range(TILES):
                    if t >= 3:
                        g.wait_ge(s_xev.h, 4 * (t - 3) + 4)
                    ds_x.issue(g, xrows[t % 3][:], x_d[t * 128:(t + 1) * 128, :])
                    for c in (2 * t, 2 * t + 1):
                        if c >= 3:
                            g.wait_ge(s_wqr.h, c - 2)
                        ds_wq.issue(g, wqst[c % 3][:], wq_d[c * 128:(c + 1) * 128, :])
                g.wait_ge(s_bx1.h, 16)
                s_bx2.inc(g.dma_start(brow[0:1, 0:128], scr_b.unsqueeze(0)), 16)

            @blk.vector
            def _(v):
                v.memset(epsc[:], EPS)
                v.wait_ge(s_cl.h, 128)
                s_bx.inc(v.reduce_sum(bqc[:], bqs[:], axis=AX.X), 1)

                for t in range(TILES):
                    for c in (2 * t, 2 * t + 1):
                        ds_wq.wait(v, c)
                        v.wait_ge(s_wqr.h, c)
                        vst(v, v.reduce_sum(
                            wqcol[:],
                            wqst[c % 3][:].rearrange("p (d h) -> p d h", h=32),
                            axis=AX.X))
                        vwt(v)
                        s_wqr.inc(v.tensor_copy(wq3[:, c, 0:128], wqcol[:]), 1)

            @blk.tensor
            def _(t_):
                t_.wait_ge(s_cl.h, 128)
                for t in range(TILES):
                    ds_x.wait(t_, t)
                    for q in range(4):
                        pq = pE if q % 2 == 0 else pF
                        t_.wait_ge(s_xev.h, max(0, 4 * t + q - 1))
                        last = None
                        for cc in range(8):
                            c = 8 * q + cc
                            last = t_.matmul(pq[:, cc * 128:(cc + 1) * 128],
                                             xrows[t % 3][:, c * 128:(c + 1) * 128],
                                             id128[:], start=True, stop=True)
                        s_xtr.inc(last)

            @blk.scalar
            def _(s):
                for t in range(TILES):
                    for q in range(4):
                        pq = pE if q % 2 == 0 else pF
                        s.wait_ge(s_xtr.h, 4 * t + q + 1)
                        dst = xT3[:, 8 * q:8 * q + 8, t * 128:(t + 1) * 128]
                        s_xev.inc(s.activation(dst, pq[:].rearrange("p (c n) -> p c n", c=8),
                                               AF.Copy))

            barrier(blk, {
                "gpsimd": [(ds_x.sems[0], 16 * 6), (ds_wq.sems[0], 16 * 11), (s_cl2, 64), (s_bx2, 16)],
                "sync": [(s_cl, 128), (s_bx1, 16)],
                "vector": [(s_wqr, 32)],
                "scalar": [(s_xev, 64)],
                "tensor": [(s_xtr, 64)],
            })
        p0.close()

        pa = ExitStack(); st.enter_context(pa)
        repl = pa.enter_context(nc.sbuf_tensor("repl", [128, D], BF16))
        qf = pa.enter_context(nc.sbuf_tensor("qf", [128, 8192], BF16))
        vi = [pa.enter_context(nc.sbuf_tensor(f"vi{i}", [128, 256], BF16)) for i in range(2)]
        pbuf = [pa.enter_context(nc.sbuf_tensor(f"pbuf{i}", [128, 1024], BF16)) for i in range(2)]
        qkr = [pa.enter_context(nc.sbuf_tensor(f"qkr{i}", [128, 256], BF16)) for i in range(2)]
        vr = [pa.enter_context(nc.sbuf_tensor(f"vr{i}", [128, 128], BF16)) for i in range(2)]
        otb = [pa.enter_context(nc.sbuf_tensor(f"otb{i}", [128, 128], BF16)) for i in range(2)]
        rbuf = pa.enter_context(nc.sbuf_tensor("rbuf", [128, 128], F32))
        pre1 = pa.enter_context(nc.sbuf_tensor("pre1", [128, D], BF16))
        x1r = pa.enter_context(nc.sbuf_tensor("x1r", [128, D], BF16))
        bn6 = pa.enter_context(nc.sbuf_tensor("bn6", [128, 48], F32))
        mv = pa.enter_context(nc.sbuf_tensor("mv", [128, 2], F32))
        sdc = pa.enter_context(nc.sbuf_tensor("sdc", [128, 1], F32))
        rstd = pa.enter_context(nc.sbuf_tensor("rstd", [128, 1], F32))
        nmr = pa.enter_context(nc.sbuf_tensor("nmr", [128, 1], F32))
        hlf = pa.enter_context(nc.sbuf_tensor("hlf", [128, 1], F32))

        with nc.Block() as blk:
            @blk.gpsimd
            def _(g):
                s_cl2.inc(g.dma_start(repl[:], repl_h.ap()), 16)
                g.memset(vi[0][:].rearrange("p (j two) -> p j two", two=2)[:, :, 1:2], 1.0)
                s_vi.inc(g.memset(
                    vi[1][:].rearrange("p (j two) -> p j two", two=2)[:, :, 1:2], 1.0))

            @blk.sync
            def _(sy):
                if dbg:
                    sy.wait_ge(s_wqr.h, 32)
                    sy.wait_ge(s_cl2.h, 64)
                    sy.wait_ge(s_bx.h, 33)
                    s_dbg.inc(sy.dma_start(dbg_wq, wqkv[:]), 16)
                    s_dbg.inc(sy.dma_start(dbg_br, brow[:]), 16)
                for t in range(TILES):
                    par = t % 2
                    sy.wait_ge(s_qke.h, 3 * t + 3)
                    if dbg:
                        s_dbg.inc(sy.dma_start(dbg_qk[t], qkr[par][:]), 16)
                    ds_fo.issue(sy, scr_qk[t], qkr[par][:])
                    for qq in range(4):
                        ds_fo.wait(sy, t)
                        sy.wait_ge(s_sg.h, max(0, 16 * t + 4 * qq))
                        rows = scr_qk[t][32 * qq:32 * qq + 32, :]
                        ds_fi.issue(
                            sy,
                            qf[0:1, 0:8192].rearrange("o (h p n) -> o h p n", h=2, p=32),
                            rows[:, 0:256].rearrange("p (h n) -> h p n", h=2).unsqueeze(0))
                    if dbg:
                        sy.wait_ge(s_dv.h, t + 1)
                        s_dbg.inc(sy.dma_start(dbg_o[t], otb[par][:]), 16)
                        sy.wait_ge(s_nm.h, t + 1)
                        s_dbg.inc(sy.dma_start(dbg_x1[t], x1r[:]), 16)

            @blk.tensor
            def _(t_):
                t_.wait_ge(s_cl.h, 128)
                t_.wait_ge(s_cl2.h, 80)
                t_.wait_ge(s_bx2.h, 16)
                t_.wait_ge(s_wqr.h, 32)

                def qkv_mm(t):
                    pq = pA if t % 2 == 0 else pB
                    if t >= 2:
                        t_.wait_ge(s_qke.h, 3 * (t - 2) + 3)
                    for c in range(NCH):
                        t_.matmul(pq[:, 0:384], xT3[:, c, t * 128:(t + 1) * 128],
                                  wq3[:, c, :], start=(c == 0), stop=False)
                    s_qkv.inc(t_.matmul(pq[:, 0:384], ones1[:], brow[:],
                                        start=False, stop=True))

                def pointwise(t):
                    par = t % 2
                    t_.wait_ge(s_qke.h, 3 * t + 3)
                    t_.wait_ge(s_vi.h, 1 + t)
                    s_vt.inc(t_.matmul(pC[:, par * 128:par * 128 + 128], vr[par][:],
                                       id128[:], start=True, stop=True))
                    if t >= 2:
                        t_.wait_ge(s_x1e.h, 4 * (t - 2))
                    if t >= 1:
                        t_.wait_ge(s_le.h, 4 * (t - 1))
                        t_.wait_ge(s_dv.h, t)
                    def sgrp(grp):
                        ps = pE if grp % 2 == 0 else pF
                        if grp % 4 == 0:
                            ds_fi.wait(t_, 4 * t + grp // 4)
                        t_.wait_ge(s_ex.h, max(0, 16 * t + grp - 1))
                        last = None
                        for j in range(8):
                            tok = (8 * grp + j) % 32
                            last = t_.matmul(ps[:, j * 128:(j + 1) * 128],
                                             qf[0:1, 4096 + tok * 128:4096 + (tok + 1) * 128],
                                             qf[0:1, tok * 128:(tok + 1) * 128],
                                             start=True, stop=True)
                        s_sg.inc(last)

                    def ndgrp(grp):
                        gp = grp % 2
                        t_.wait_ge(s_ex.h, 16 * t + grp + 1)
                        t_.wait_ge(s_vi.h, 2 + t)
                        last = None
                        for j in range(8):
                            tok = 8 * grp + j
                            off = par * 256 + 2 * tok
                            last = t_.matmul(pD[:, off:off + 2],
                                             pbuf[gp][:, j * 128:(j + 1) * 128],
                                             vi[par][:, 2 * tok:2 * tok + 2],
                                             start=True, stop=True)
                        s_nd.inc(last)

                    sgrp(0)
                    sgrp(1)
                    for grp in range(16):
                        ndgrp(grp)
                        if grp + 2 < 16:
                            sgrp(grp + 2)

                def ln1(t):
                    par = t % 2
                    t_.wait_ge(s_dv.h, t + 1)
                    t_.wait_ge(s_ex.h, 16 * min(t + 2, TILES))
                    if t >= 1:
                        t_.wait_ge(s_x1e.h, 4 * t)
                    for piece in range(4):
                        pp = pE if piece % 2 == 0 else pF
                        if piece >= 2:
                            t_.wait_ge(s_le.h, 4 * t + piece - 1)
                        last = None
                        for half in range(2):
                            col0 = piece * 1024 + half * 512
                            last = t_.matmul(pp[:, half * 512:half * 512 + 512],
                                             otb[par][:], repl[:, col0:col0 + 512],
                                             start=True, stop=False)
                            for cc in range(4 * half, 4 * half + 4):
                                c = piece * 8 + cc
                                last = t_.matmul(pp[:, cc * 128:(cc + 1) * 128],
                                                 xT3[:, c, t * 128:(t + 1) * 128], id128[:],
                                                 start=False, stop=(cc % 4 == 3))
                        s_ln.inc(last)

                def x1t(t, ln_guard=None):
                    t_.wait_ge(s_nm.h, t + 1)
                    if ln_guard is not None:
                        t_.wait_ge(s_le.h, 4 * ln_guard)
                    for q in range(4):
                        pq = pE if q % 2 == 0 else pF
                        t_.wait_ge(s_x1e.h, max(0, 4 * t + q - 1))
                        last = None
                        for cc in range(8):
                            c = 8 * q + cc
                            last = t_.matmul(pq[:, cc * 128:(cc + 1) * 128],
                                             x1r[:, c * 128:(c + 1) * 128], id128[:],
                                             start=True, stop=True)
                        s_x1.inc(last)

                qkv_mm(0)
                for t in range(TILES):
                    if t + 1 < TILES:
                        qkv_mm(t + 1)
                    pointwise(t)
                    if t >= 2:
                        x1t(t - 2)
                    if t >= 1:
                        ln1(t - 1)
                x1t(TILES - 2, ln_guard=TILES - 1)
                ln1(TILES - 1)
                x1t(TILES - 1, ln_guard=TILES)

            @blk.scalar
            def _(s):
                def act_evac_qkv(u):
                    par = u % 2
                    pq = pA if par == 0 else pB
                    s.wait_ge(s_qkv.h, u + 1)
                    if u >= 2:
                        ds_fo.wait(s, u - 2)
                        s.wait_ge(s_vt.h, u - 1)
                    s.activation(qkr[par][:, 0:128], pq[:, 0:128], AF.Copy, scale=SCALE_Q)
                    s.activation(qkr[par][:, 128:256], pq[:, 128:256], AF.Copy)
                    s_qke.inc(s.activation(vr[par][:], pq[:, 256:384], AF.Copy), 3)

                def act_exp(t):
                    for grp in range(16):
                        gp = grp % 2
                        ps = pE if gp == 0 else pF
                        s.wait_ge(s_sg.h, 16 * t + grp + 1)
                        s.wait_ge(s_nd.h, max(0, 16 * t + grp - 1))
                        s_ex.inc(s.activation(pbuf[gp][:], ps[:], AF.Exp))

                def act_stats(t):
                    s.wait_ge(s_stt.h, 5 * t + 1)
                    s_stt.inc(s.activation(sdc[:], mv[:, 1:2], AF.Ln, bias=epsc[:]))
                    s.wait_ge(s_stt.h, 5 * t + 3)
                    s_stt.inc(s.activation(rstd[:], hlf[:], AF.Exp))

                act_evac_qkv(0)
                for t in range(TILES):
                    if t + 1 < TILES:
                        act_evac_qkv(t + 1)
                    act_exp(t)
                    if t >= 1:
                        act_stats(t - 1)
                act_stats(TILES - 1)

            @blk.vector
            def _(v):
                def dve_pw(t):
                    par = t % 2
                    v.wait_ge(s_vt.h, t + 1)
                    if t >= 2:
                        v.wait_ge(s_nd.h, 16 * (t - 1))
                    s_vi.inc(v.tensor_copy(
                        vi[par][:].rearrange("p (j two) -> p j two", two=2)[:, :, 0:1],
                        pC[:, par * 128:par * 128 + 128].unsqueeze(2)))
                    v.wait_ge(s_nd.h, 16 * (t + 1))
                    nd3 = pD[:, par * 256:par * 256 + 256].rearrange(
                        "p (j two) -> p j two", two=2)
                    vst(v, v.reciprocal(rbuf[:], nd3[:, :, 1:2].squeeze(2)))
                    vwt(v)
                    s_dv.inc(v.tensor_tensor(otb[par][:], nd3[:, :, 0:1].squeeze(2),
                                             rbuf[:], ALU.mult))

                def dve_lnev(t):
                    for piece in range(4):
                        pp = pE if piece % 2 == 0 else pF
                        v.wait_ge(s_ln.h, 4 * t + piece + 1)
                        if t >= 1 and piece == 0:
                            v.wait_ge(s_stt.h, 5 * (t - 1) + 1)
                        s_le.inc(v.tensor_copy(pre1[:, piece * 1024:(piece + 1) * 1024],
                                               pp[:]))

                def dve_x1e(t):
                    for q in range(4):
                        v.wait_ge(s_x1.h, 4 * t + q + 1)
                        pq2 = pE if q % 2 == 0 else pF
                        dst = xT3[:, 8 * q:8 * q + 8, t * 128:(t + 1) * 128]
                        s_x1e.inc(v.tensor_copy(dst, pq2[:].rearrange("p (c n) -> p c n",
                                                                      c=8)))

                def dve_stats(t):
                    v.wait_ge(s_le.h, 4 * (t + 1))
                    if t >= 1:
                        v.wait_ge(s_nm.h, t)
                    for i in range(8):
                        vst(v, v.bn_stats(bn6[:, 6 * i:6 * i + 6],
                                          pre1[:, 512 * i:512 * i + 512]))
                    vwt(v)
                    s_stt.inc(v.bn_aggr(mv[:], bn6[:]))
                    v.wait_ge(s_stt.h, 5 * t + 2)
                    s_stt.inc(v.tensor_scalar_mul(hlf[:], sdc[:], -0.5))
                    v.wait_ge(s_stt.h, 5 * t + 4)
                    s_stt.inc(v.scalar_tensor_tensor(nmr[:], mv[:, 0:1], -1.0, rstd[:],
                                                     ALU.mult, ALU.mult))
                    v.wait_ge(s_stt.h, 5 * t + 5)
                    if t >= 1:
                        v.wait_ge(s_x1.h, 4 * t)
                    s_nm.inc(v.tensor_scalar(out=x1r[:], in0=pre1[:], scalar1=rstd[:],
                                             scalar2=nmr[:], op0=ALU.mult, op1=ALU.add))

                for t in range(TILES):
                    dve_pw(t)
                    if t >= 2:
                        dve_x1e(t - 2)
                    if t >= 1:
                        dve_lnev(t - 1)
                        dve_stats(t - 1)
                dve_x1e(TILES - 2)
                dve_lnev(TILES - 1)
                dve_stats(TILES - 1)
                dve_x1e(TILES - 1)

            barrier(blk, {
                "gpsimd": [(s_cl2, 80), (s_vi, 1)],
                "sync": [(ds_fo.sems[0], 16 * 8), (ds_fo.sems[1], 16 * 8), (ds_fi.sems[0], 16 * 32), (ds_fi.sems[1], 16 * 32)],
                "vector": [(s_stt, 5 * TILES), (s_x1e, 4 * TILES), (s_nm, TILES)],
                "scalar": [(s_ex, 16 * TILES)],
                "tensor": [(s_x1, 4 * TILES)],
            })
        pa.close()
        p01.close()

        fc = ExitStack(); st.enter_context(fc)
        slab = [fc.enter_context(nc.sbuf_tensor(f"slab{i}", [128, NCH * SLABW], BF16))
                for i in range(2)]
        btb = fc.enter_context(nc.sbuf_tensor("btb", [128, D], BF16))
        bbr = [fc.enter_context(nc.sbuf_tensor(f"bbr{i}", [1, SLABW], BF16)) for i in range(2)]

        def load_slab(g, si):
            par = si % 2
            sl3 = slab[par][:].rearrange("p (c n) -> p c n", c=NCH)
            for dd in range(4):
                src = wt_d[dd * 1024:(dd + 1) * 1024, si * SLABW:(si + 1) * SLABW]
                ds_sl.issue(g, sl3[:, 8 * dd:8 * dd + 8, :],
                            src.rearrange("(c p) n -> p c n", p=128))
            ds_bb.issue(g, bbr[par][:],
                        bb_d[si * SLABW:(si + 1) * SLABW].unsqueeze(0))

        b0 = ExitStack(); st.enter_context(b0)
        hT = b0.enter_context(nc.sbuf_tensor("hT", [128, NT], BF16))
        atb = b0.enter_context(nc.sbuf_tensor("atb", [128, NCH * 128], BF16))
        rwb = b0.enter_context(nc.sbuf_tensor("rwb", [128, NCH * 16], BF16))
        lgb = b0.enter_context(nc.sbuf_tensor("lgb", [15, NT], BF16))
        erow = b0.enter_context(nc.sbuf_tensor("erow", [128, 15], F32))
        prob = b0.enter_context(nc.sbuf_tensor("prob", [128, 15], F32))
        rank = b0.enter_context(nc.sbuf_tensor("rank", [128, 15], F32))
        selm = b0.enter_context(nc.sbuf_tensor("selm", [128, 15], F32))
        wcb = b0.enter_context(nc.sbuf_tensor("wcb", [128, 15], BF16))
        wctb = b0.enter_context(nc.sbuf_tensor("wctb", [15, 128], BF16))
        w8b = b0.enter_context(nc.sbuf_tensor("w8b", [8, 128], BF16))
        sj = b0.enter_context(nc.sbuf_tensor("sj", [128, 16 * 15], BF16))
        nmx = b0.enter_context(nc.sbuf_tensor("nmx", [128, 1], F32))
        sme = b0.enter_context(nc.sbuf_tensor("sme", [128, 1], F32))
        rcp = b0.enter_context(nc.sbuf_tensor("rcp", [128, 1], F32))
        a3 = atb[:].rearrange("p (c n) -> p c n", c=NCH)
        r3 = rwb[:].rearrange("p (c n) -> p c n", c=NCH)

        with nc.Block() as blk:
            @blk.gpsimd
            def _(g):
                s_b0d.inc(g.dma_start(a3, at_d.rearrange("(c p) n -> p c n", p=128)), 16)
                s_b0d.inc(g.dma_start(r3[:, :, 0:15],
                                      rwt_d.rearrange("(c p) n -> p c n", p=128)), 16)
                s_cl2.inc(g.dma_start(btb[:], bt_d), 16)
                load_slab(g, 0)
                load_slab(g, 1)

            @blk.tensor
            def _(t_):
                t_.wait_ge(s_b0d.h, 32)
                for q in range(4):
                    pq = pE if q < 2 else pF
                    cols = (q % 2) * 512
                    last = None
                    for c in range(NCH):
                        last = t_.matmul(pq[:, cols:cols + 512], a3[:, c, :],
                                         xT3[:, c, q * 512:(q + 1) * 512],
                                         start=(c == 0), stop=(c == NCH - 1))
                s_b0m.inc(last)
                for q in range(4):
                    pq = (pA, pB, pC, pD)[q]
                    last = None
                    for c in range(NCH):
                        last = t_.matmul(pq[0:15, 0:512], r3[:, c, 0:15],
                                         xT3[:, c, q * 512:(q + 1) * 512],
                                         start=(c == 0), stop=(c == NCH - 1))
                s_b0m.inc(last)
                for t in range(TILES):
                    t_.wait_ge(s_b0e.h, 3)
                    t_.wait_ge(s_pr.h, 1)
                    if t >= 1:
                        t_.wait_ge(s_wce.h, t)
                    s_lgt.inc(t_.matmul(pC[:, 256:271], lgb[:, t * 128:(t + 1) * 128],
                                        id15[:], start=True, stop=True))
                    t_.wait_ge(s_pr.h, 1 + 2 * t + 2)
                    s_wct.inc(t_.matmul(pC[0:15, 272:400], wcb[:], id128[:],
                                        start=True, stop=True))
                    t_.wait_ge(s_wce.h, t + 1)
                    if t >= 1:
                        t_.wait_ge(s_hw.h, t)
                    s_w8.inc(t_.matmul(pD[0:8, 256:384], ohb[:], wctb[:],
                                       start=True, stop=True))
                    t_.wait_ge(s_w8e.h, t + 1)
                    s_w128.inc(t_.matmul(pD[:, 384:512], r16[:], w8b[:],
                                         start=True, stop=True))

            @blk.scalar
            def _(s):
                s.wait_ge(s_b0m.h, 1)
                s.activation(hT[:, 0:512], pE[:, 0:512], AF.Copy)
                s.activation(hT[:, 512:1024], pE[:, 512:1024], AF.Copy)
                s.activation(hT[:, 1024:1536], pF[:, 0:512], AF.Copy)
                s_b0e.inc(s.activation(hT[:, 1536:2048], pF[:, 512:1024], AF.Copy))
                s.wait_ge(s_b0m.h, 2)
                for q in range(4):
                    pq = (pA, pB, pC, pD)[q]
                    ins = s.activation(lgb[:, q * 512:(q + 1) * 512], pq[0:15, 0:512],
                                       AF.Identity, bias=rbc[:])
                s_b0e.inc(ins, 2)
                for t in range(TILES):
                    s.wait_ge(s_pr.h, 1 + 2 * t + 1)
                    if t >= 1:
                        s.wait_ge(s_pr.h, 1 + 2 * (t - 1) + 2)
                    s_exr.inc(s.activation(erow[:], pC[:, 256:271], AF.Exp, bias=nmx[:]))
                    s.wait_ge(s_wct.h, t + 1)
                    s_wce.inc(s.activation(wctb[:], pC[0:15, 272:400], AF.Copy))
                    s.wait_ge(s_w8.h, t + 1)
                    s_w8e.inc(s.activation(w8b[:], pD[0:8, 256:384], AF.Copy, scale=2.0))

            @blk.vector
            def _(v):
                v.wait_ge(s_cl.h, 128)
                vst(v, v.tensor_copy(emf[:], emc[:]))
                vwt(v)
                vst(v, v.tensor_scalar(out=selm[0:15, 0:8], in0=iota[:], scalar1=emf[:],
                                       scalar2=0.0, op0=ALU.is_equal, op1=ALU.add))
                vwt(v)
                s_pr.inc(v.tensor_copy(ohb[:], selm[0:15, 0:8]))
                for t in range(TILES):
                    v.wait_ge(s_lgt.h, t + 1)
                    if t >= 1:
                        v.wait_ge(s_exr.h, t)
                    s_pr.inc(v.reduce_max(nmx[:], pC[:, 256:271], axis=AX.X,
                                          negate=True))
                    v.wait_ge(s_exr.h, t + 1)
                    vst(v, v.reduce_sum(sme[:], erow[:], axis=AX.X))
                    vwt(v)
                    vst(v, v.reciprocal(rcp[:], sme[:]))
                    vwt(v)
                    vst(v, v.tensor_scalar(out=prob[:], in0=erow[:], scalar1=rcp[:],
                                           scalar2=None, op0=ALU.mult))
                    vwt(v)
                    for j in range(15):
                        ins = v.tensor_scalar(out=sj[:, 15 * j:15 * j + 15], in0=prob[:],
                                              scalar1=prob[:, j:j + 1],
                                              scalar2=0.0, op0=ALU.is_gt, op1=ALU.add,
                                              accum_out=rank[:, j:j + 1])
                        if j == 14:
                            vst(v, ins)
                    vwt(v)
                    vst(v, v.tensor_scalar(out=selm[:], in0=rank[:], scalar1=7.5,
                                           scalar2=None, op0=ALU.is_lt))
                    vwt(v)
                    s_pr.inc(v.tensor_tensor(wcb[:], prob[:], selm[:], ALU.mult))
                    v.wait_ge(s_w128.h, t + 1)
                    s_hw.inc(v.tensor_tensor(hwT[:, t * 128:(t + 1) * 128],
                                             hT[:, t * 128:(t + 1) * 128],
                                             pD[:, 384:512], ALU.mult))

            @blk.sync
            def _(sy):
                if dbg:
                    for t in range(TILES):
                        sy.wait_ge(s_pr.h, 1 + 2 * t + 2)
                        s_dbg.inc(sy.dma_start(dbg_pr[t], prob[:]), 16)
                        sy.wait_ge(s_w8e.h, t + 1)
                        s_dbg.inc(sy.dma_start(dbg_w[t], w8b[:]), 16)
                    sy.wait_ge(s_hw.h, 16)
                    s_dbg.inc(sy.dma_start(dbg_hw, hwT[:]), 16)
                    sy.wait_ge(s_dbg.h, s_dbg.n)

            barrier(blk, {
                "gpsimd": [(s_b0d, 32)],
                "sync": [],
                "vector": [(s_hw, TILES)],
                "scalar": [(s_w8e, TILES)],
                "tensor": [(s_w128, TILES)],
            })
        b0.close()

        fc2 = ExitStack(); st.enter_context(fc2)
        outb = [fc2.enter_context(nc.sbuf_tensor(f"outb{i}", [128, SLABW], BF16))
                for i in range(2)]
        bnf = fc2.enter_context(nc.sbuf_tensor("bnf", [128, TILES * 96], F32))
        pre2 = [fc2.enter_context(nc.sbuf_tensor("pre2_0", [128, D], BF16))]
        our = [fc2.enter_context(nc.sbuf_tensor(f"our{i}", [128, 2048], F32))
               for i in range(2)]
        mv2 = fc2.enter_context(nc.sbuf_tensor("mv2", [128, 2], F32))
        sd2 = fc2.enter_context(nc.sbuf_tensor("sd2", [128, 1], F32))
        rs2 = fc2.enter_context(nc.sbuf_tensor("rs2", [128, 1], F32))
        nm2 = fc2.enter_context(nc.sbuf_tensor("nm2", [128, 1], F32))
        hl2 = fc2.enter_context(nc.sbuf_tensor("hl2", [128, 1], F32))

        with nc.Block() as blk:
            @blk.gpsimd
            def _(g):
                for si in range(2, NSLAB):
                    g.wait_ge(s_fm.h, TILES * (si - 1))
                    load_slab(g, si)

            @blk.tensor
            def _(t_):
                t_.wait_ge(s_hw.h, TILES)
                t_.wait_ge(s_cl2.h, 96)
                for si in range(NSLAB):
                    par = si % 2
                    ds_sl.wait_all_upto(t_, 4 * si + 3)
                    ds_bb.wait(t_, si)
                    sl3 = slab[par][:].rearrange("p (c n) -> p c n", c=NCH)
                    for t in range(TILES):
                        k = si * TILES + t
                        pq = pA if k % 2 == 0 else pB
                        if k >= 2:
                            t_.wait_ge(s_fe.h, k - 1)
                        for c in range(NCH):
                            t_.matmul(pq[:, 0:SLABW], xT3[:, c, t * 128:(t + 1) * 128],
                                      sl3[:, c, :], start=(c == 0), stop=False)
                        for cc in range(2):
                            c = 2 * si + cc
                            t_.matmul(pq[:, cc * 128:(cc + 1) * 128],
                                      xT3[:, c, t * 128:(t + 1) * 128], id128[:],
                                      start=False, stop=False)
                        t_.matmul(pq[:, 0:SLABW], hwT[:, t * 128:(t + 1) * 128],
                                  btb[:, si * SLABW:(si + 1) * SLABW],
                                  start=False, stop=False)
                        s_fm.inc(t_.matmul(pq[:, 0:SLABW], ones1[:], bbr[par][:],
                                           start=False, stop=True))

            @blk.scalar
            def _(s):
                for si in range(NSLAB):
                    for t in range(TILES):
                        k = si * TILES + t
                        pq = pA if k % 2 == 0 else pB
                        s.wait_ge(s_fm.h, k + 1)
                        if k >= 2:
                            ds_sp.wait(s, k - 2)
                            s.wait_ge(s_fb.h, k - 1)
                        s_fe.inc(s.activation(outb[k % 2][:], pq[:, 0:SLABW], AF.Copy))
                        if si == NSLAB - 1:
                            s.wait_ge(s_pcs.h, 5 * t + 1)
                            s_pcs.inc(s.activation(sd2[:], mv2[:, 1:2], AF.Ln,
                                                   bias=epsc[:]))
                            s.wait_ge(s_pcs.h, 5 * t + 3)
                            s_pcs.inc(s.activation(rs2[:], hl2[:], AF.Exp))

            @blk.vector
            def _(v):
                for si in range(NSLAB):
                    for t in range(TILES):
                        k = si * TILES + t
                        v.wait_ge(s_fe.h, k + 1)
                        s_fb.inc(v.bn_stats(bnf[:, t * 96 + si * 6:t * 96 + si * 6 + 6],
                                            outb[k % 2][:]))
                        if si == NSLAB - 1:
                            v.wait_ge(s_fb.h, k + 1)
                            if t >= 1:
                                v.wait_ge(s_pcn.h, 2 * t)
                            s_pcs.inc(v.bn_aggr(mv2[:], bnf[:, t * 96:t * 96 + 96]))
                            v.wait_ge(s_pcs.h, 5 * t + 2)
                            s_pcs.inc(v.tensor_scalar_mul(hl2[:], sd2[:], -0.5))
                            v.wait_ge(s_pcs.h, 5 * t + 4)
                            s_pcs.inc(v.scalar_tensor_tensor(nm2[:], mv2[:, 0:1], -1.0,
                                                             rs2[:], ALU.mult, ALU.mult))
                            v.wait_ge(s_pcs.h, 5 * t + 5)
                            ds_p2.wait(v, t)
                            for half in range(2):
                                hb = (2 * t + half) % 2
                                if 2 * t + half >= 2:
                                    ds_y.wait(v, 2 * t + half - 2)
                                s_pcn.inc(v.tensor_scalar(
                                    out=our[hb][:],
                                    in0=pre2[0][:, half * 2048:(half + 1) * 2048],
                                    scalar1=rs2[:], scalar2=nm2[:],
                                    op0=ALU.mult, op1=ALU.add))

            @blk.sync
            def _(sy):
                for si in range(NSLAB):
                    for t in range(TILES):
                        k = si * TILES + t
                        sy.wait_ge(s_fe.h, k + 1)
                        ds_sp.issue(sy,
                                    pre2_d[t * 128:(t + 1) * 128,
                                           si * SLABW:(si + 1) * SLABW],
                                    outb[k % 2][:])
                        if si == NSLAB - 1:
                            ds_sp.wait_all_upto(sy, (NSLAB - 1) * TILES + t)
                            if t >= 1:
                                sy.wait_ge(s_pcn.h, 2 * t)
                            ds_p2.issue(sy, pre2[0][:],
                                        pre2_d[t * 128:(t + 1) * 128, :])
                            for half in range(2):
                                sy.wait_ge(s_pcn.h, 2 * t + half + 1)
                                ds_y.issue(sy,
                                           y_d[t * 128:(t + 1) * 128,
                                               half * 2048:(half + 1) * 2048],
                                           our[(2 * t + half) % 2][:])
                ds_y.wait_done(sy)

            barrier(blk, {
                "gpsimd": [(ds_sl.sems[0], 16 * 32), (ds_sl.sems[1], 16 * 32),
                           (ds_bb.sems[0], 16 * 8), (ds_bb.sems[1], 16 * 8)],
                "sync": [],
                "vector": [(s_pcs, 5 * TILES), (s_pcn, 2 * TILES)],
                "scalar": [(s_exr, 0)],
                "tensor": [(s_fm, NSLAB * TILES)],
            })
        fc2.close()
        fc.close()

        assert s_qkv.n == TILES and s_qke.n == 3 * TILES, (s_qkv.n, s_qke.n)
        assert s_sg.n == 16 * TILES and s_ex.n == 16 * TILES and s_nd.n == 16 * TILES
        assert s_dv.n == TILES
        assert s_ln.n == 4 * TILES and s_le.n == 4 * TILES
        assert s_x1.n == 4 * TILES and s_x1e.n == 4 * TILES
        assert s_stt.n == 5 * TILES and s_nm.n == TILES
        assert s_fm.n == NSLAB * TILES and s_fe.n == NSLAB * TILES
        assert s_fb.n == NSLAB * TILES
        assert s_pcs.n == 5 * TILES and s_pcn.n == 2 * TILES
        assert ds_y.i == 2 * TILES and ds_sp.i == NSLAB * TILES

    return nc


def _stage(inputs):
    f = np.ascontiguousarray
    x = np.asarray(inputs["L0_attn_x"], np.float32).reshape(16384, D)
    shared = dict(
        wq=np.asarray(inputs["L0_attn_Wq"], np.float32).reshape(D, D),
        wk=np.asarray(inputs["L0_attn_Wk"], np.float32).reshape(D, 128),
        wv=np.asarray(inputs["L0_attn_Wv"], np.float32).reshape(D, 128),
        bq=np.asarray(inputs["L0_attn_bq"], np.float32),
        bk=np.asarray(inputs["L0_attn_bk"], np.float32).reshape(128),
        bv=np.asarray(inputs["L0_attn_bv"], np.float32).reshape(128),
        wt=f(np.asarray(inputs["L0_ffn_base_W"], np.float32).T),
        bb=np.asarray(inputs["L0_ffn_base_b"], np.float32),
        rwt=f(np.asarray(inputs["L0_ffn_router_W"], np.float32).T),
        rb=np.asarray(inputs["L0_ffn_router_b"], np.float32),
        at=f(np.asarray(inputs["L0_ffn_A"], np.float32).reshape(128, D).T),
        bt=f(np.asarray(inputs["L0_ffn_B"], np.float32).transpose(0, 2, 1).reshape(128, D)),
        emap=np.asarray(inputs["L0_ffn_expert_map"], np.int32),
    )
    return x, shared


def kernel(**inputs):
    x, shared = _stage(inputs)
    nc = build_program()
    in_maps = [dict(shared, x=np.ascontiguousarray(x[c * NT:(c + 1) * NT]))
               for c in range(8)]
    res = run_bass_kernel_spmd(nc, in_maps, core_ids=list(range(8)))
    out = np.concatenate([np.asarray(res.results[c]["y"]) for c in range(8)], axis=0)
    return out.reshape(2, 8192, D).astype(np.float32)


# version 10
# speedup vs baseline: 1.2974x; 1.0224x over previous
from contextlib import ExitStack

import numpy as np
import ml_dtypes
import concourse.bass as bass
import concourse.mybir as mybir
from concourse.bass_utils import run_bass_kernel_spmd

F32 = mybir.dt.float32
BF16 = mybir.dt.bfloat16
I32 = mybir.dt.int32
AF = mybir.ActivationFunctionType
ALU = mybir.AluOpType
AX = mybir.AxisListType

NT = 2048
TILES = 16
NCH = 32
D = 4096
SCALE_Q = 1.0 / float(np.sqrt(128.0))
NSLAB = 16
SLABW = 256
EPS = 1e-5


class Sem:
    def __init__(self, nc, st, name):
        self.h = st.enter_context(nc.semaphore(name))
        self.n = 0

    def inc(self, instr, amount=1):
        instr.then_inc(self.h, amount)
        self.n += amount
        return self.n


class DStream:

    def __init__(self, nc, st, name, r=2):
        self.sems = [Sem(nc, st, f"{name}{i}") for i in range(r)]
        self.r = r
        self.i = 0

    def issue(self, eng, out, in_, **kw):
        p = self.i % self.r
        if self.i >= self.r:
            eng.wait_ge(self.sems[p].h, 16 * (self.i // self.r))
        self.sems[p].inc(eng.dma_start(out, in_, **kw), 16)
        self.i += 1
        return self.i - 1

    def wait(self, eng, idx):
        p = idx % self.r
        eng.wait_ge(self.sems[p].h, 16 * (idx // self.r + 1))

    def wait_all_upto(self, eng, idx):
        for p in range(self.r):
            last = idx - ((idx - p) % self.r)
            if last >= 0:
                eng.wait_ge(self.sems[p].h, 16 * (last // self.r + 1))

    def wait_done(self, eng):
        for p in range(self.r):
            last = self.i - 1 - ((self.i - 1 - p) % self.r)
            if last >= 0:
                eng.wait_ge(self.sems[p].h, 16 * (last // self.r + 1))


def build_program(dbg=False):
    nc = bass.Bass("TRN2", target_bir_lowering=False)

    x_d = nc.dram_tensor("x", [NT, D], F32, kind="ExternalInput").ap()
    wq_d = nc.dram_tensor("wq", [D, D], F32, kind="ExternalInput").ap()
    wk_d = nc.dram_tensor("wk", [D, 128], F32, kind="ExternalInput").ap()
    wv_d = nc.dram_tensor("wv", [D, 128], F32, kind="ExternalInput").ap()
    bq_d = nc.dram_tensor("bq", [128, 32], F32, kind="ExternalInput").ap()
    bk_d = nc.dram_tensor("bk", [128], F32, kind="ExternalInput").ap()
    bv_d = nc.dram_tensor("bv", [128], F32, kind="ExternalInput").ap()
    wt_d = nc.dram_tensor("wt", [D, D], F32, kind="ExternalInput").ap()
    bb_d = nc.dram_tensor("bb", [D], F32, kind="ExternalInput").ap()
    rwt_d = nc.dram_tensor("rwt", [D, 15], F32, kind="ExternalInput").ap()
    rb_d = nc.dram_tensor("rb", [15], F32, kind="ExternalInput").ap()
    at_d = nc.dram_tensor("at", [D, 128], F32, kind="ExternalInput").ap()
    bt_d = nc.dram_tensor("bt", [128, D], F32, kind="ExternalInput").ap()
    em_d = nc.dram_tensor("emap", [15], I32, kind="ExternalInput").ap()
    y_d = nc.dram_tensor("y", [NT, D], F32, kind="ExternalOutput").ap()

    scr_qk = nc.dram_tensor("scr_qk", [TILES, 128, 256], BF16, kind="Internal").ap()
    scr_b = nc.dram_tensor("scr_b", [128], F32, kind="Internal").ap()
    pre2_d = nc.dram_tensor("pre2", [NT, D], BF16, kind="Internal").ap()

    if dbg:
        dbg_o = nc.dram_tensor("dbg_o", [TILES, 128, 128], BF16, kind="ExternalOutput").ap()
        dbg_x1 = nc.dram_tensor("dbg_x1", [TILES, 128, D], BF16, kind="ExternalOutput").ap()
        dbg_pr = nc.dram_tensor("dbg_pr", [TILES, 128, 15], F32, kind="ExternalOutput").ap()
        dbg_w = nc.dram_tensor("dbg_w", [TILES, 8, 128], BF16, kind="ExternalOutput").ap()
        dbg_hw = nc.dram_tensor("dbg_hw", [128, NT], BF16, kind="ExternalOutput").ap()
        dbg_wq = nc.dram_tensor("dbg_wq", [128, NCH * 384], BF16, kind="ExternalOutput").ap()
        dbg_br = nc.dram_tensor("dbg_br", [1, 384], BF16, kind="ExternalOutput").ap()
        dbg_qk = nc.dram_tensor("dbg_qk", [TILES, 128, 256], BF16, kind="ExternalOutput").ap()

    id128_h = nc.inline_tensor(np.eye(128, dtype=np.float32).astype(ml_dtypes.bfloat16), "c_id128")
    id15_h = nc.inline_tensor(np.eye(15, dtype=np.float32).astype(ml_dtypes.bfloat16), "c_id15")
    repl_np = np.zeros((128, D), np.float32)
    repl_np[np.arange(D) // 32, np.arange(D)] = 1.0
    repl_h = nc.inline_tensor(repl_np.astype(ml_dtypes.bfloat16), "c_repl")
    r16_np = np.zeros((8, 128), np.float32)
    r16_np[np.arange(128) // 16, np.arange(128)] = 1.0
    r16_h = nc.inline_tensor(r16_np.astype(ml_dtypes.bfloat16), "c_r16")
    iota_h = nc.inline_tensor(np.tile(np.arange(8, dtype=np.float32), (15, 1)), "c_iota")
    ones_h = nc.inline_tensor(np.ones((1, 128), np.float32).astype(ml_dtypes.bfloat16), "c_ones")

    st = ExitStack()
    with st:
        E = st.enter_context

        xT = E(nc.sbuf_tensor("xT", [128, NCH * NT], BF16))
        id128 = E(nc.sbuf_tensor("id128", [128, 128], BF16))
        id15 = E(nc.sbuf_tensor("id15", [15, 15], BF16))
        ones1 = E(nc.sbuf_tensor("ones1", [1, 128], BF16))
        brow = E(nc.sbuf_tensor("brow", [1, 384], BF16))
        r16 = E(nc.sbuf_tensor("r16", [8, 128], BF16))
        iota = E(nc.sbuf_tensor("iota", [15, 8], F32))
        emc = E(nc.sbuf_tensor("emc", [15, 1], I32))
        emf = E(nc.sbuf_tensor("emf", [15, 1], F32))
        ohb = E(nc.sbuf_tensor("ohb", [15, 8], BF16))
        rbc = E(nc.sbuf_tensor("rbc", [15, 1], F32))
        epsc = E(nc.sbuf_tensor("epsc", [128, 1], F32))
        hwT = E(nc.sbuf_tensor("hwT", [128, NT], BF16))

        pA = E(nc.psum_tensor("pA", [128, 512], F32))
        pB = E(nc.psum_tensor("pB", [128, 512], F32))
        pC = E(nc.psum_tensor("pC", [128, 512], F32))
        pD = E(nc.psum_tensor("pD", [128, 512], F32))
        pE = E(nc.psum_tensor("pE", [128, 1024], F32))
        pF = E(nc.psum_tensor("pF", [128, 1024], F32))

        sst = ExitStack()
        st.enter_context(sst)
        mk = lambda name: Sem(nc, sst, name)
        s_cl = mk("cl"); s_xl = mk("xl"); s_xtr = mk("xtr"); s_xev = mk("xev")
        s_wql = mk("wql"); s_wqr = mk("wqr"); s_bx = mk("bx")
        s_qkv = mk("qkv"); s_qke = mk("qke"); s_fl = mk("fl"); s_vt = mk("vt")
        s_vi = mk("vi"); s_sg = mk("sg"); s_ex = mk("ex"); s_nd = mk("nd")
        s_dv = mk("dv"); s_ln = mk("ln"); s_le = mk("le"); s_stt = mk("stt")
        s_nm = mk("nm"); s_x1 = mk("x1"); s_x1e = mk("x1e")
        s_b0m = mk("b0m"); s_b0e = mk("b0e"); s_lgt = mk("lgt"); s_pr = mk("pr")
        s_exr = mk("exr"); s_wct = mk("wct"); s_wce = mk("wce"); s_w8 = mk("w8")
        s_w8e = mk("w8e"); s_w128 = mk("w128"); s_hw = mk("hw")
        s_wl = mk("wl"); s_bl = mk("bl"); s_fm = mk("fm"); s_fe = mk("fe")
        s_fb = mk("fb"); s_sp = mk("sp"); s_p2 = mk("p2"); s_pcs = mk("pcs")
        s_pcn = mk("pcn"); s_out = mk("out"); s_ph = mk("ph"); s_cl2 = mk("cl2"); s_dbg = mk("dbg")
        s_vv = mk("vv"); s_bx1 = mk("bx1"); s_bx2 = mk("bx2"); s_b0d = mk("b0d"); s_xc = mk("xc")
        ds_x = DStream(nc, sst, "dsx", 3)
        ds_wq = DStream(nc, sst, "dswq", 3)
        ds_fo = DStream(nc, sst, "dsfo")
        ds_fi = DStream(nc, sst, "dsfi")
        ds_sl = DStream(nc, sst, "dssl")
        ds_bb = DStream(nc, sst, "dsbb")
        ds_sp = DStream(nc, sst, "dssp")
        ds_p2 = DStream(nc, sst, "dsp2")
        ds_y = DStream(nc, sst, "dsy")

        def vst(v, ins):
            s_vv.inc(ins)

        def vwt(v):
            v.wait_ge(s_vv.h, s_vv.n)

        xT3 = xT[:].rearrange("p (c n) -> p c n", c=NCH)

        PH = [0]

        def barrier(blk, guards):
            PH[0] += 1
            tgt = 5 * PH[0]
            for name in ("gpsimd", "sync", "vector", "scalar", "tensor"):
                def _f(eng, _g=guards.get(name, ()), _t=tgt):
                    for sem, val in _g:
                        eng.wait_ge(sem.h, val)
                    eng.nop().then_inc(s_ph.h, 1)
                    eng.wait_ge(s_ph.h, _t)
                getattr(blk, name)(_f)
            s_ph.n = tgt

        p01 = ExitStack(); st.enter_context(p01)
        wqkv = p01.enter_context(nc.sbuf_tensor("wqkv", [128, NCH * 384], BF16))
        p0 = ExitStack(); st.enter_context(p0)
        xrows = [p0.enter_context(nc.sbuf_tensor(f"xrows{i}", [128, D], BF16)) for i in range(3)]
        wqst = [p0.enter_context(nc.sbuf_tensor(f"wqst{i}", [128, D], BF16)) for i in range(3)]
        wq3 = wqkv[:].rearrange("p (c n) -> p c n", c=NCH)
        wqcol = p0.enter_context(nc.sbuf_tensor("wqcol", [128, 128], F32))
        bqs = p0.enter_context(nc.sbuf_tensor("bqs", [128, 32], F32))
        bqc = p0.enter_context(nc.sbuf_tensor("bqc", [128, 1], F32))

        with nc.Block() as blk:
            @blk.sync
            def _(sy):
                s_cl.inc(sy.dma_start(id128[:], id128_h.ap()), 16)
                s_cl.inc(sy.dma_start(id15[:], id15_h.ap()), 16)
                s_cl.inc(sy.dma_start(ones1[:], ones_h.ap()), 16)
                s_cl.inc(sy.dma_start(r16[:], r16_h.ap()), 16)
                s_cl.inc(sy.dma_start(iota[:], iota_h.ap()), 16)
                s_cl.inc(sy.dma_start(emc[:], em_d.unsqueeze(1)), 16)
                s_cl.inc(sy.dma_start(rbc[:], rb_d.unsqueeze(1)), 16)
                s_cl.inc(sy.dma_start(bqs[:], bq_d), 16)
                sy.wait_ge(s_bx.h, 1)
                s_bx1.inc(sy.dma_start(scr_b.unsqueeze(1), bqc[:]), 16)

            @blk.gpsimd
            def _(g):
                s_cl2.inc(g.dma_start(wq3[:, :, 128:256],
                                      wk_d.rearrange("(c p) n -> p c n", p=128)), 16)
                s_cl2.inc(g.dma_start(wq3[:, :, 256:384],
                                      wv_d.rearrange("(c p) n -> p c n", p=128)), 16)
                s_cl2.inc(g.dma_start(brow[0:1, 128:256], bk_d.unsqueeze(0)), 16)
                s_cl2.inc(g.dma_start(brow[0:1, 256:384], bv_d.unsqueeze(0)), 16)
                for t in range(TILES):
                    if t >= 3:
                        g.wait_ge(s_xev.h, 4 * (t - 3) + 4)
                    ds_x.issue(g, xrows[t % 3][:], x_d[t * 128:(t + 1) * 128, :])
                    for c in (2 * t, 2 * t + 1):
                        if c >= 3:
                            g.wait_ge(s_wqr.h, c - 2)
                        ds_wq.issue(g, wqst[c % 3][:], wq_d[c * 128:(c + 1) * 128, :])
                g.wait_ge(s_bx1.h, 16)
                s_bx2.inc(g.dma_start(brow[0:1, 0:128], scr_b.unsqueeze(0)), 16)

            @blk.vector
            def _(v):
                v.memset(epsc[:], EPS)
                v.wait_ge(s_cl.h, 128)
                s_bx.inc(v.reduce_sum(bqc[:], bqs[:], axis=AX.X), 1)

                for t in range(TILES):
                    for c in (2 * t, 2 * t + 1):
                        ds_wq.wait(v, c)
                        v.wait_ge(s_wqr.h, c)
                        vst(v, v.reduce_sum(
                            wqcol[:],
                            wqst[c % 3][:].rearrange("p (d h) -> p d h", h=32),
                            axis=AX.X))
                        vwt(v)
                        s_wqr.inc(v.tensor_copy(wq3[:, c, 0:128], wqcol[:]), 1)

            @blk.tensor
            def _(t_):
                t_.wait_ge(s_cl.h, 128)
                for t in range(TILES):
                    ds_x.wait(t_, t)
                    for q in range(4):
                        pq = pE if q % 2 == 0 else pF
                        t_.wait_ge(s_xev.h, max(0, 4 * t + q - 1))
                        last = None
                        for cc in range(8):
                            c = 8 * q + cc
                            last = t_.matmul(pq[:, cc * 128:(cc + 1) * 128],
                                             xrows[t % 3][:, c * 128:(c + 1) * 128],
                                             id128[:], start=True, stop=True)
                        s_xtr.inc(last)

            @blk.scalar
            def _(s):
                for t in range(TILES):
                    for q in range(4):
                        pq = pE if q % 2 == 0 else pF
                        s.wait_ge(s_xtr.h, 4 * t + q + 1)
                        dst = xT3[:, 8 * q:8 * q + 8, t * 128:(t + 1) * 128]
                        s_xev.inc(s.activation(dst, pq[:].rearrange("p (c n) -> p c n", c=8),
                                               AF.Copy))

            barrier(blk, {
                "gpsimd": [(ds_x.sems[0], 16 * 6), (ds_wq.sems[0], 16 * 11), (s_cl2, 64), (s_bx2, 16)],
                "sync": [(s_cl, 128), (s_bx1, 16)],
                "vector": [(s_wqr, 32)],
                "scalar": [(s_xev, 64)],
                "tensor": [(s_xtr, 64)],
            })
        p0.close()

        pa = ExitStack(); st.enter_context(pa)
        repl = pa.enter_context(nc.sbuf_tensor("repl", [128, D], BF16))
        qf = pa.enter_context(nc.sbuf_tensor("qf", [128, 8192], BF16))
        vi = [pa.enter_context(nc.sbuf_tensor(f"vi{i}", [128, 256], BF16)) for i in range(2)]
        pbuf = [pa.enter_context(nc.sbuf_tensor(f"pbuf{i}", [128, 1024], BF16)) for i in range(2)]
        qkr = [pa.enter_context(nc.sbuf_tensor(f"qkr{i}", [128, 256], BF16)) for i in range(2)]
        vr = [pa.enter_context(nc.sbuf_tensor(f"vr{i}", [128, 128], BF16)) for i in range(2)]
        otb = [pa.enter_context(nc.sbuf_tensor(f"otb{i}", [128, 128], BF16)) for i in range(2)]
        rbuf = pa.enter_context(nc.sbuf_tensor("rbuf", [128, 128], F32))
        pre1 = pa.enter_context(nc.sbuf_tensor("pre1", [128, D], BF16))
        x1r = pa.enter_context(nc.sbuf_tensor("x1r", [128, D], BF16))
        bn6 = pa.enter_context(nc.sbuf_tensor("bn6", [128, 48], F32))
        mv = pa.enter_context(nc.sbuf_tensor("mv", [128, 2], F32))
        sdc = pa.enter_context(nc.sbuf_tensor("sdc", [128, 1], F32))
        rstd = pa.enter_context(nc.sbuf_tensor("rstd", [128, 1], F32))
        nmr = pa.enter_context(nc.sbuf_tensor("nmr", [128, 1], F32))
        hlf = pa.enter_context(nc.sbuf_tensor("hlf", [128, 1], F32))

        with nc.Block() as blk:
            @blk.gpsimd
            def _(g):
                s_cl2.inc(g.dma_start(repl[:], repl_h.ap()), 16)
                g.memset(vi[0][:].rearrange("p (j two) -> p j two", two=2)[:, :, 1:2], 1.0)
                s_vi.inc(g.memset(
                    vi[1][:].rearrange("p (j two) -> p j two", two=2)[:, :, 1:2], 1.0))

            @blk.sync
            def _(sy):
                if dbg:
                    sy.wait_ge(s_wqr.h, 32)
                    sy.wait_ge(s_cl2.h, 64)
                    sy.wait_ge(s_bx.h, 33)
                    s_dbg.inc(sy.dma_start(dbg_wq, wqkv[:]), 16)
                    s_dbg.inc(sy.dma_start(dbg_br, brow[:]), 16)
                for t in range(TILES):
                    par = t % 2
                    sy.wait_ge(s_qke.h, 3 * t + 3)
                    if dbg:
                        s_dbg.inc(sy.dma_start(dbg_qk[t], qkr[par][:]), 16)
                    ds_fo.issue(sy, scr_qk[t], qkr[par][:])
                    for qq in range(4):
                        ds_fo.wait(sy, t)
                        sy.wait_ge(s_sg.h, max(0, 16 * t + 4 * qq))
                        rows = scr_qk[t][32 * qq:32 * qq + 32, :]
                        ds_fi.issue(
                            sy,
                            qf[0:1, 0:8192].rearrange("o (h p n) -> o h p n", h=2, p=32),
                            rows[:, 0:256].rearrange("p (h n) -> h p n", h=2).unsqueeze(0))
                    if dbg:
                        sy.wait_ge(s_dv.h, t + 1)
                        s_dbg.inc(sy.dma_start(dbg_o[t], otb[par][:]), 16)
                        sy.wait_ge(s_nm.h, t + 1)
                        s_dbg.inc(sy.dma_start(dbg_x1[t], x1r[:]), 16)

            @blk.tensor
            def _(t_):
                t_.wait_ge(s_cl.h, 128)
                t_.wait_ge(s_cl2.h, 80)
                t_.wait_ge(s_bx2.h, 16)
                t_.wait_ge(s_wqr.h, 32)

                def qkv_mm(t):
                    pq = pA if t % 2 == 0 else pB
                    if t >= 2:
                        t_.wait_ge(s_qke.h, 3 * (t - 2) + 3)
                    for c in range(NCH):
                        t_.matmul(pq[:, 0:384], xT3[:, c, t * 128:(t + 1) * 128],
                                  wq3[:, c, :], start=(c == 0), stop=False)
                    s_qkv.inc(t_.matmul(pq[:, 0:384], ones1[:], brow[:],
                                        start=False, stop=True))

                def pointwise(t):
                    par = t % 2
                    t_.wait_ge(s_qke.h, 3 * t + 3)
                    t_.wait_ge(s_vi.h, 1 + t)
                    s_vt.inc(t_.matmul(pC[:, par * 128:par * 128 + 128], vr[par][:],
                                       id128[:], start=True, stop=True))
                    if t >= 2:
                        t_.wait_ge(s_x1e.h, 4 * (t - 2))
                    if t >= 1:
                        t_.wait_ge(s_le.h, 4 * (t - 1))
                        t_.wait_ge(s_dv.h, t)
                    def sgrp(grp):
                        ps = pE if grp % 2 == 0 else pF
                        if grp % 4 == 0:
                            ds_fi.wait(t_, 4 * t + grp // 4)
                        t_.wait_ge(s_ex.h, max(0, 16 * t + grp - 1))
                        last = None
                        for j in range(8):
                            tok = (8 * grp + j) % 32
                            last = t_.matmul(ps[:, j * 128:(j + 1) * 128],
                                             qf[0:1, 4096 + tok * 128:4096 + (tok + 1) * 128],
                                             qf[0:1, tok * 128:(tok + 1) * 128],
                                             start=True, stop=True)
                        s_sg.inc(last)

                    def ndgrp(grp):
                        gp = grp % 2
                        t_.wait_ge(s_ex.h, 16 * t + grp + 1)
                        t_.wait_ge(s_vi.h, 2 + t)
                        last = None
                        for j in range(8):
                            tok = 8 * grp + j
                            off = par * 256 + 2 * tok
                            last = t_.matmul(pD[:, off:off + 2],
                                             pbuf[gp][:, j * 128:(j + 1) * 128],
                                             vi[par][:, 2 * tok:2 * tok + 2],
                                             start=True, stop=True)
                        s_nd.inc(last)

                    sgrp(0)
                    sgrp(1)
                    for grp in range(16):
                        ndgrp(grp)
                        if grp + 2 < 16:
                            sgrp(grp + 2)

                def ln1(t):
                    par = t % 2
                    t_.wait_ge(s_dv.h, t + 1)
                    t_.wait_ge(s_ex.h, 16 * min(t + 2, TILES))
                    if t >= 1:
                        t_.wait_ge(s_x1e.h, 4 * t)
                    for piece in range(4):
                        pp = pE if piece % 2 == 0 else pF
                        if piece >= 2:
                            t_.wait_ge(s_le.h, 4 * t + piece - 1)
                        last = None
                        for half in range(2):
                            col0 = piece * 1024 + half * 512
                            last = t_.matmul(pp[:, half * 512:half * 512 + 512],
                                             otb[par][:], repl[:, col0:col0 + 512],
                                             start=True, stop=False)
                            for cc in range(4 * half, 4 * half + 4):
                                c = piece * 8 + cc
                                last = t_.matmul(pp[:, cc * 128:(cc + 1) * 128],
                                                 xT3[:, c, t * 128:(t + 1) * 128], id128[:],
                                                 start=False, stop=(cc % 4 == 3))
                        s_ln.inc(last)

                def x1t(t, ln_guard=None):
                    t_.wait_ge(s_nm.h, t + 1)
                    if ln_guard is not None:
                        t_.wait_ge(s_le.h, 4 * ln_guard)
                    for q in range(4):
                        pq = pE if q % 2 == 0 else pF
                        t_.wait_ge(s_x1e.h, max(0, 4 * t + q - 1))
                        last = None
                        for cc in range(8):
                            c = 8 * q + cc
                            last = t_.matmul(pq[:, cc * 128:(cc + 1) * 128],
                                             x1r[:, c * 128:(c + 1) * 128], id128[:],
                                             start=True, stop=True)
                        s_x1.inc(last)

                qkv_mm(0)
                for t in range(TILES):
                    if t + 1 < TILES:
                        qkv_mm(t + 1)
                    pointwise(t)
                    if t >= 2:
                        x1t(t - 2)
                    if t >= 1:
                        ln1(t - 1)
                x1t(TILES - 2, ln_guard=TILES - 1)
                ln1(TILES - 1)
                x1t(TILES - 1, ln_guard=TILES)

            @blk.scalar
            def _(s):
                def act_evac_qkv(u):
                    par = u % 2
                    pq = pA if par == 0 else pB
                    s.wait_ge(s_qkv.h, u + 1)
                    if u >= 2:
                        ds_fo.wait(s, u - 2)
                        s.wait_ge(s_vt.h, u - 1)
                    s.activation(qkr[par][:, 0:128], pq[:, 0:128], AF.Copy, scale=SCALE_Q)
                    s.activation(qkr[par][:, 128:256], pq[:, 128:256], AF.Copy)
                    s_qke.inc(s.activation(vr[par][:], pq[:, 256:384], AF.Copy), 3)

                def act_exp(t):
                    for grp in range(16):
                        gp = grp % 2
                        ps = pE if gp == 0 else pF
                        s.wait_ge(s_sg.h, 16 * t + grp + 1)
                        s.wait_ge(s_nd.h, max(0, 16 * t + grp - 1))
                        s_ex.inc(s.activation(pbuf[gp][:], ps[:], AF.Exp))

                def act_stats(t):
                    s.wait_ge(s_stt.h, 5 * t + 1)
                    s_stt.inc(s.activation(sdc[:], mv[:, 1:2], AF.Ln, bias=epsc[:]))
                    s.wait_ge(s_stt.h, 5 * t + 3)
                    s_stt.inc(s.activation(rstd[:], hlf[:], AF.Exp))

                act_evac_qkv(0)
                for t in range(TILES):
                    if t + 1 < TILES:
                        act_evac_qkv(t + 1)
                    act_exp(t)
                    if t >= 1:
                        act_stats(t - 1)
                act_stats(TILES - 1)

            @blk.vector
            def _(v):
                def dve_pw(t):
                    par = t % 2
                    v.wait_ge(s_vt.h, t + 1)
                    if t >= 2:
                        v.wait_ge(s_nd.h, 16 * (t - 1))
                    s_vi.inc(v.tensor_copy(
                        vi[par][:].rearrange("p (j two) -> p j two", two=2)[:, :, 0:1],
                        pC[:, par * 128:par * 128 + 128].unsqueeze(2)))
                    v.wait_ge(s_nd.h, 16 * (t + 1))
                    nd3 = pD[:, par * 256:par * 256 + 256].rearrange(
                        "p (j two) -> p j two", two=2)
                    vst(v, v.reciprocal(rbuf[:], nd3[:, :, 1:2].squeeze(2)))
                    vwt(v)
                    s_dv.inc(v.tensor_tensor(otb[par][:], nd3[:, :, 0:1].squeeze(2),
                                             rbuf[:], ALU.mult))

                def dve_lnev(t):
                    for piece in range(4):
                        pp = pE if piece % 2 == 0 else pF
                        v.wait_ge(s_ln.h, 4 * t + piece + 1)
                        if t >= 1 and piece == 0:
                            v.wait_ge(s_stt.h, 5 * (t - 1) + 1)
                        s_le.inc(v.tensor_copy(pre1[:, piece * 1024:(piece + 1) * 1024],
                                               pp[:]))

                def dve_x1e(t):
                    for q in range(4):
                        v.wait_ge(s_x1.h, 4 * t + q + 1)
                        pq2 = pE if q % 2 == 0 else pF
                        dst = xT3[:, 8 * q:8 * q + 8, t * 128:(t + 1) * 128]
                        s_x1e.inc(v.tensor_copy(dst, pq2[:].rearrange("p (c n) -> p c n",
                                                                      c=8)))

                def dve_stats(t):
                    v.wait_ge(s_le.h, 4 * (t + 1))
                    if t >= 1:
                        v.wait_ge(s_nm.h, t)
                    for i in range(8):
                        vst(v, v.bn_stats(bn6[:, 6 * i:6 * i + 6],
                                          pre1[:, 512 * i:512 * i + 512]))
                    vwt(v)
                    s_stt.inc(v.bn_aggr(mv[:], bn6[:]))
                    v.wait_ge(s_stt.h, 5 * t + 2)
                    s_stt.inc(v.tensor_scalar_mul(hlf[:], sdc[:], -0.5))
                    v.wait_ge(s_stt.h, 5 * t + 4)
                    s_stt.inc(v.scalar_tensor_tensor(nmr[:], mv[:, 0:1], -1.0, rstd[:],
                                                     ALU.mult, ALU.mult))
                    v.wait_ge(s_stt.h, 5 * t + 5)
                    if t >= 1:
                        v.wait_ge(s_x1.h, 4 * t)
                    s_nm.inc(v.tensor_scalar(out=x1r[:], in0=pre1[:], scalar1=rstd[:],
                                             scalar2=nmr[:], op0=ALU.mult, op1=ALU.add))

                for t in range(TILES):
                    dve_pw(t)
                    if t >= 2:
                        dve_x1e(t - 2)
                    if t >= 1:
                        dve_lnev(t - 1)
                        dve_stats(t - 1)
                dve_x1e(TILES - 2)
                dve_lnev(TILES - 1)
                dve_stats(TILES - 1)
                dve_x1e(TILES - 1)

            barrier(blk, {
                "gpsimd": [(s_cl2, 80), (s_vi, 1)],
                "sync": [(ds_fo.sems[0], 16 * 8), (ds_fo.sems[1], 16 * 8), (ds_fi.sems[0], 16 * 32), (ds_fi.sems[1], 16 * 32)],
                "vector": [(s_stt, 5 * TILES), (s_x1e, 4 * TILES), (s_nm, TILES)],
                "scalar": [(s_ex, 16 * TILES)],
                "tensor": [(s_x1, 4 * TILES)],
            })
        pa.close()
        p01.close()

        fc = ExitStack(); st.enter_context(fc)
        slab = [fc.enter_context(nc.sbuf_tensor(f"slab{i}", [128, NCH * SLABW], BF16))
                for i in range(2)]
        btb = fc.enter_context(nc.sbuf_tensor("btb", [128, D], BF16))
        bbr = [fc.enter_context(nc.sbuf_tensor(f"bbr{i}", [1, SLABW], BF16)) for i in range(2)]

        def load_slab(g, si):
            par = si % 2
            sl3 = slab[par][:].rearrange("p (c n) -> p c n", c=NCH)
            for dd in range(4):
                src = wt_d[dd * 1024:(dd + 1) * 1024, si * SLABW:(si + 1) * SLABW]
                ds_sl.issue(g, sl3[:, 8 * dd:8 * dd + 8, :],
                            src.rearrange("(c p) n -> p c n", p=128))
            ds_bb.issue(g, bbr[par][:],
                        bb_d[si * SLABW:(si + 1) * SLABW].unsqueeze(0))

        b0 = ExitStack(); st.enter_context(b0)
        hT = b0.enter_context(nc.sbuf_tensor("hT", [128, NT], BF16))
        atb = b0.enter_context(nc.sbuf_tensor("atb", [128, NCH * 128], BF16))
        rwb = b0.enter_context(nc.sbuf_tensor("rwb", [128, NCH * 16], BF16))
        lgb = b0.enter_context(nc.sbuf_tensor("lgb", [15, NT], BF16))
        erow = b0.enter_context(nc.sbuf_tensor("erow", [128, 15], F32))
        prob = b0.enter_context(nc.sbuf_tensor("prob", [128, 15], F32))
        rank = b0.enter_context(nc.sbuf_tensor("rank", [128, 15], F32))
        selm = b0.enter_context(nc.sbuf_tensor("selm", [128, 15], F32))
        wcb = b0.enter_context(nc.sbuf_tensor("wcb", [128, 15], BF16))
        wctb = b0.enter_context(nc.sbuf_tensor("wctb", [15, 128], BF16))
        w8b = b0.enter_context(nc.sbuf_tensor("w8b", [8, 128], BF16))
        sj = b0.enter_context(nc.sbuf_tensor("sj", [128, 16 * 15], BF16))
        nmx = b0.enter_context(nc.sbuf_tensor("nmx", [128, 1], F32))
        sme = b0.enter_context(nc.sbuf_tensor("sme", [128, 1], F32))
        rcp = b0.enter_context(nc.sbuf_tensor("rcp", [128, 1], F32))
        a3 = atb[:].rearrange("p (c n) -> p c n", c=NCH)
        r3 = rwb[:].rearrange("p (c n) -> p c n", c=NCH)

        with nc.Block() as blk:
            @blk.gpsimd
            def _(g):
                s_b0d.inc(g.dma_start(a3, at_d.rearrange("(c p) n -> p c n", p=128)), 16)
                s_b0d.inc(g.dma_start(r3[:, :, 0:15],
                                      rwt_d.rearrange("(c p) n -> p c n", p=128)), 16)
                s_cl2.inc(g.dma_start(btb[:], bt_d), 16)
                load_slab(g, 0)
                load_slab(g, 1)

            @blk.tensor
            def _(t_):
                t_.wait_ge(s_b0d.h, 32)
                for q in range(4):
                    pq = pE if q < 2 else pF
                    cols = (q % 2) * 512
                    last = None
                    for c in range(NCH):
                        last = t_.matmul(pq[:, cols:cols + 512], a3[:, c, :],
                                         xT3[:, c, q * 512:(q + 1) * 512],
                                         start=(c == 0), stop=(c == NCH - 1))
                s_b0m.inc(last)
                for q in range(4):
                    pq = (pA, pB, pC, pD)[q]
                    last = None
                    for c in range(NCH):
                        last = t_.matmul(pq[0:15, 0:512], r3[:, c, 0:15],
                                         xT3[:, c, q * 512:(q + 1) * 512],
                                         start=(c == 0), stop=(c == NCH - 1))
                s_b0m.inc(last)
                for t in range(TILES):
                    t_.wait_ge(s_b0e.h, 3)
                    t_.wait_ge(s_pr.h, 1)
                    if t >= 1:
                        t_.wait_ge(s_wce.h, t)
                    s_lgt.inc(t_.matmul(pC[:, 256:271], lgb[:, t * 128:(t + 1) * 128],
                                        id15[:], start=True, stop=True))
                    t_.wait_ge(s_pr.h, 1 + 2 * t + 2)
                    s_wct.inc(t_.matmul(pC[0:15, 272:400], wcb[:], id128[:],
                                        start=True, stop=True))
                    t_.wait_ge(s_wce.h, t + 1)
                    if t >= 1:
                        t_.wait_ge(s_hw.h, t)
                    s_w8.inc(t_.matmul(pD[0:8, 256:384], ohb[:], wctb[:],
                                       start=True, stop=True))
                    t_.wait_ge(s_w8e.h, t + 1)
                    s_w128.inc(t_.matmul(pD[:, 384:512], r16[:], w8b[:],
                                         start=True, stop=True))

            @blk.scalar
            def _(s):
                s.wait_ge(s_b0m.h, 1)
                s.activation(hT[:, 0:512], pE[:, 0:512], AF.Copy)
                s.activation(hT[:, 512:1024], pE[:, 512:1024], AF.Copy)
                s.activation(hT[:, 1024:1536], pF[:, 0:512], AF.Copy)
                s_b0e.inc(s.activation(hT[:, 1536:2048], pF[:, 512:1024], AF.Copy))
                s.wait_ge(s_b0m.h, 2)
                for q in range(4):
                    pq = (pA, pB, pC, pD)[q]
                    ins = s.activation(lgb[:, q * 512:(q + 1) * 512], pq[0:15, 0:512],
                                       AF.Identity, bias=rbc[:])
                s_b0e.inc(ins, 2)
                for t in range(TILES):
                    s.wait_ge(s_pr.h, 1 + 2 * t + 1)
                    if t >= 1:
                        s.wait_ge(s_pr.h, 1 + 2 * (t - 1) + 2)
                    s_exr.inc(s.activation(erow[:], pC[:, 256:271], AF.Exp, bias=nmx[:]))
                    s.wait_ge(s_wct.h, t + 1)
                    s_wce.inc(s.activation(wctb[:], pC[0:15, 272:400], AF.Copy))
                    s.wait_ge(s_w8.h, t + 1)
                    s_w8e.inc(s.activation(w8b[:], pD[0:8, 256:384], AF.Copy, scale=2.0))

            @blk.vector
            def _(v):
                v.wait_ge(s_cl.h, 128)
                vst(v, v.tensor_copy(emf[:], emc[:]))
                vwt(v)
                vst(v, v.tensor_scalar(out=selm[0:15, 0:8], in0=iota[:], scalar1=emf[:],
                                       scalar2=0.0, op0=ALU.is_equal, op1=ALU.add))
                vwt(v)
                s_pr.inc(v.tensor_copy(ohb[:], selm[0:15, 0:8]))
                for t in range(TILES):
                    v.wait_ge(s_lgt.h, t + 1)
                    if t >= 1:
                        v.wait_ge(s_exr.h, t)
                    s_pr.inc(v.reduce_max(nmx[:], pC[:, 256:271], axis=AX.X,
                                          negate=True))
                    v.wait_ge(s_exr.h, t + 1)
                    vst(v, v.reduce_sum(sme[:], erow[:], axis=AX.X))
                    vwt(v)
                    vst(v, v.reciprocal(rcp[:], sme[:]))
                    vwt(v)
                    vst(v, v.tensor_scalar(out=prob[:], in0=erow[:], scalar1=rcp[:],
                                           scalar2=None, op0=ALU.mult))
                    vwt(v)
                    for j in range(15):
                        ins = v.tensor_scalar(out=sj[:, 15 * j:15 * j + 15], in0=prob[:],
                                              scalar1=prob[:, j:j + 1],
                                              scalar2=0.0, op0=ALU.is_gt, op1=ALU.add,
                                              accum_out=rank[:, j:j + 1])
                        if j == 14:
                            vst(v, ins)
                    vwt(v)
                    vst(v, v.tensor_scalar(out=selm[:], in0=rank[:], scalar1=7.5,
                                           scalar2=None, op0=ALU.is_lt))
                    vwt(v)
                    s_pr.inc(v.tensor_tensor(wcb[:], prob[:], selm[:], ALU.mult))
                    v.wait_ge(s_w128.h, t + 1)
                    s_hw.inc(v.tensor_tensor(hwT[:, t * 128:(t + 1) * 128],
                                             hT[:, t * 128:(t + 1) * 128],
                                             pD[:, 384:512], ALU.mult))

            @blk.sync
            def _(sy):
                if dbg:
                    for t in range(TILES):
                        sy.wait_ge(s_pr.h, 1 + 2 * t + 2)
                        s_dbg.inc(sy.dma_start(dbg_pr[t], prob[:]), 16)
                        sy.wait_ge(s_w8e.h, t + 1)
                        s_dbg.inc(sy.dma_start(dbg_w[t], w8b[:]), 16)
                    sy.wait_ge(s_hw.h, 16)
                    s_dbg.inc(sy.dma_start(dbg_hw, hwT[:]), 16)
                    sy.wait_ge(s_dbg.h, s_dbg.n)

        b0.close()

        fc2 = ExitStack(); st.enter_context(fc2)
        outb = [fc2.enter_context(nc.sbuf_tensor(f"outb{i}", [128, SLABW], BF16))
                for i in range(2)]
        bnf = fc2.enter_context(nc.sbuf_tensor("bnf", [128, TILES * 96], F32))
        pre2 = [fc2.enter_context(nc.sbuf_tensor("pre2_0", [128, D], BF16))]
        our = [fc2.enter_context(nc.sbuf_tensor(f"our{i}", [128, 2048], F32))
               for i in range(2)]
        mv2 = fc2.enter_context(nc.sbuf_tensor("mv2", [128, 2], F32))
        sd2 = fc2.enter_context(nc.sbuf_tensor("sd2", [128, 1], F32))
        rs2 = fc2.enter_context(nc.sbuf_tensor("rs2", [128, 1], F32))
        nm2 = fc2.enter_context(nc.sbuf_tensor("nm2", [128, 1], F32))
        hl2 = fc2.enter_context(nc.sbuf_tensor("hl2", [128, 1], F32))

        with nc.Block() as blk:
            @blk.gpsimd
            def _(g):
                for si in range(2, NSLAB):
                    g.wait_ge(s_fm.h, TILES * (si - 1))
                    load_slab(g, si)

            @blk.tensor
            def _(t_):
                t_.wait_ge(s_b0e.h, 3)
                t_.wait_ge(s_cl2.h, 96)
                for si in range(NSLAB):
                    par = si % 2
                    ds_sl.wait_all_upto(t_, 4 * si + 3)
                    ds_bb.wait(t_, si)
                    sl3 = slab[par][:].rearrange("p (c n) -> p c n", c=NCH)
                    for t in range(TILES):
                        k = si * TILES + t
                        pq = pA if k % 2 == 0 else pB
                        if k >= 2:
                            t_.wait_ge(s_fe.h, k - 1)
                        for c in range(NCH):
                            t_.matmul(pq[:, 0:SLABW], xT3[:, c, t * 128:(t + 1) * 128],
                                      sl3[:, c, :], start=(c == 0), stop=False)
                        for cc in range(2):
                            c = 2 * si + cc
                            t_.matmul(pq[:, cc * 128:(cc + 1) * 128],
                                      xT3[:, c, t * 128:(t + 1) * 128], id128[:],
                                      start=False, stop=False)
                        if si == 0:
                            t_.wait_ge(s_hw.h, t + 1)
                        t_.matmul(pq[:, 0:SLABW], hwT[:, t * 128:(t + 1) * 128],
                                  btb[:, si * SLABW:(si + 1) * SLABW],
                                  start=False, stop=False)
                        s_fm.inc(t_.matmul(pq[:, 0:SLABW], ones1[:], bbr[par][:],
                                           start=False, stop=True))

            @blk.scalar
            def _(s):
                s.wait_ge(s_hw.h, TILES)
                for si in range(NSLAB):
                    for t in range(TILES):
                        k = si * TILES + t
                        pq = pA if k % 2 == 0 else pB
                        s.wait_ge(s_fm.h, k + 1)
                        if k >= 2:
                            ds_sp.wait(s, k - 2)
                            s.wait_ge(s_fb.h, k - 1)
                        s_fe.inc(s.activation(outb[k % 2][:], pq[:, 0:SLABW], AF.Copy))
                        if si == NSLAB - 1:
                            s.wait_ge(s_pcs.h, 5 * t + 1)
                            s_pcs.inc(s.activation(sd2[:], mv2[:, 1:2], AF.Ln,
                                                   bias=epsc[:]))
                            s.wait_ge(s_pcs.h, 5 * t + 3)
                            s_pcs.inc(s.activation(rs2[:], hl2[:], AF.Exp))
                            for half in range(2):
                                s.wait_ge(s_pcn.h, 2 * t + half + 1)
                                ds_y.issue(s,
                                           y_d[t * 128:(t + 1) * 128,
                                               half * 2048:(half + 1) * 2048],
                                           our[(2 * t + half) % 2][:])

            @blk.vector
            def _(v):
                for si in range(NSLAB):
                    for t in range(TILES):
                        k = si * TILES + t
                        v.wait_ge(s_fe.h, k + 1)
                        s_fb.inc(v.bn_stats(bnf[:, t * 96 + si * 6:t * 96 + si * 6 + 6],
                                            outb[k % 2][:]))
                        if si == NSLAB - 1:
                            v.wait_ge(s_fb.h, k + 1)
                            if t >= 1:
                                v.wait_ge(s_pcn.h, 2 * t)
                            s_pcs.inc(v.bn_aggr(mv2[:], bnf[:, t * 96:t * 96 + 96]))
                            v.wait_ge(s_pcs.h, 5 * t + 2)
                            s_pcs.inc(v.tensor_scalar_mul(hl2[:], sd2[:], -0.5))
                            v.wait_ge(s_pcs.h, 5 * t + 4)
                            s_pcs.inc(v.scalar_tensor_tensor(nm2[:], mv2[:, 0:1], -1.0,
                                                             rs2[:], ALU.mult, ALU.mult))
                            v.wait_ge(s_pcs.h, 5 * t + 5)
                            ds_p2.wait(v, t)
                            for half in range(2):
                                hb = (2 * t + half) % 2
                                if 2 * t + half >= 2:
                                    ds_y.wait(v, 2 * t + half - 2)
                                s_pcn.inc(v.tensor_scalar(
                                    out=our[hb][:],
                                    in0=pre2[0][:, half * 2048:(half + 1) * 2048],
                                    scalar1=rs2[:], scalar2=nm2[:],
                                    op0=ALU.mult, op1=ALU.add))

            @blk.sync
            def _(sy):
                for si in range(NSLAB):
                    for t in range(TILES):
                        k = si * TILES + t
                        sy.wait_ge(s_fe.h, k + 1)
                        ds_sp.issue(sy,
                                    pre2_d[t * 128:(t + 1) * 128,
                                           si * SLABW:(si + 1) * SLABW],
                                    outb[k % 2][:])
                        if si == NSLAB - 1:
                            ds_sp.wait_all_upto(sy, (NSLAB - 1) * TILES + t)
                            if t >= 1:
                                sy.wait_ge(s_pcn.h, 2 * t)
                            ds_p2.issue(sy, pre2[0][:],
                                        pre2_d[t * 128:(t + 1) * 128, :])
                ds_y.wait_done(sy)

            barrier(blk, {
                "gpsimd": [(ds_sl.sems[0], 16 * 32), (ds_sl.sems[1], 16 * 32),
                           (ds_bb.sems[0], 16 * 8), (ds_bb.sems[1], 16 * 8)],
                "sync": [],
                "vector": [(s_pcs, 5 * TILES), (s_pcn, 2 * TILES)],
                "scalar": [(s_exr, 0)],
                "tensor": [(s_fm, NSLAB * TILES)],
            })
        fc2.close()
        fc.close()

        assert s_qkv.n == TILES and s_qke.n == 3 * TILES, (s_qkv.n, s_qke.n)
        assert s_sg.n == 16 * TILES and s_ex.n == 16 * TILES and s_nd.n == 16 * TILES
        assert s_dv.n == TILES
        assert s_ln.n == 4 * TILES and s_le.n == 4 * TILES
        assert s_x1.n == 4 * TILES and s_x1e.n == 4 * TILES
        assert s_stt.n == 5 * TILES and s_nm.n == TILES
        assert s_fm.n == NSLAB * TILES and s_fe.n == NSLAB * TILES
        assert s_fb.n == NSLAB * TILES
        assert s_pcs.n == 5 * TILES and s_pcn.n == 2 * TILES
        assert ds_y.i == 2 * TILES and ds_sp.i == NSLAB * TILES

    return nc


def _stage(inputs):
    f = np.ascontiguousarray
    x = np.asarray(inputs["L0_attn_x"], np.float32).reshape(16384, D)
    shared = dict(
        wq=np.asarray(inputs["L0_attn_Wq"], np.float32).reshape(D, D),
        wk=np.asarray(inputs["L0_attn_Wk"], np.float32).reshape(D, 128),
        wv=np.asarray(inputs["L0_attn_Wv"], np.float32).reshape(D, 128),
        bq=np.asarray(inputs["L0_attn_bq"], np.float32),
        bk=np.asarray(inputs["L0_attn_bk"], np.float32).reshape(128),
        bv=np.asarray(inputs["L0_attn_bv"], np.float32).reshape(128),
        wt=f(np.asarray(inputs["L0_ffn_base_W"], np.float32).T),
        bb=np.asarray(inputs["L0_ffn_base_b"], np.float32),
        rwt=f(np.asarray(inputs["L0_ffn_router_W"], np.float32).T),
        rb=np.asarray(inputs["L0_ffn_router_b"], np.float32),
        at=f(np.asarray(inputs["L0_ffn_A"], np.float32).reshape(128, D).T),
        bt=f(np.asarray(inputs["L0_ffn_B"], np.float32).transpose(0, 2, 1).reshape(128, D)),
        emap=np.asarray(inputs["L0_ffn_expert_map"], np.int32),
    )
    return x, shared


def kernel(**inputs):
    x, shared = _stage(inputs)
    nc = build_program()
    in_maps = [dict(shared, x=np.ascontiguousarray(x[c * NT:(c + 1) * NT]))
               for c in range(8)]
    res = run_bass_kernel_spmd(nc, in_maps, core_ids=list(range(8)))
    out = np.concatenate([np.asarray(res.results[c]["y"]) for c in range(8)], axis=0)
    return out.reshape(2, 8192, D).astype(np.float32)


# version 13 (final)
# speedup vs baseline: 1.3153x; 1.0138x over previous
from contextlib import ExitStack

import numpy as np
import ml_dtypes
import concourse.bass as bass
import concourse.mybir as mybir
from concourse.bass_utils import run_bass_kernel_spmd

F32 = mybir.dt.float32
BF16 = mybir.dt.bfloat16
I32 = mybir.dt.int32
AF = mybir.ActivationFunctionType
ALU = mybir.AluOpType
AX = mybir.AxisListType

NT = 2048
TILES = 16
NCH = 32
D = 4096
SCALE_Q = 1.0 / float(np.sqrt(128.0))
NSLAB = 16
SLABW = 256
EPS = 1e-5


class Sem:
    def __init__(self, nc, st, name):
        self.h = st.enter_context(nc.semaphore(name))
        self.n = 0

    def inc(self, instr, amount=1):
        instr.then_inc(self.h, amount)
        self.n += amount
        return self.n


class DStream:

    def __init__(self, nc, st, name, r=2):
        self.sems = [Sem(nc, st, f"{name}{i}") for i in range(r)]
        self.r = r
        self.i = 0

    def issue(self, eng, out, in_, **kw):
        p = self.i % self.r
        if self.i >= self.r:
            eng.wait_ge(self.sems[p].h, 16 * (self.i // self.r))
        self.sems[p].inc(eng.dma_start(out, in_, **kw), 16)
        self.i += 1
        return self.i - 1

    def wait(self, eng, idx):
        p = idx % self.r
        eng.wait_ge(self.sems[p].h, 16 * (idx // self.r + 1))

    def wait_all_upto(self, eng, idx):
        for p in range(self.r):
            last = idx - ((idx - p) % self.r)
            if last >= 0:
                eng.wait_ge(self.sems[p].h, 16 * (last // self.r + 1))

    def wait_done(self, eng):
        for p in range(self.r):
            last = self.i - 1 - ((self.i - 1 - p) % self.r)
            if last >= 0:
                eng.wait_ge(self.sems[p].h, 16 * (last // self.r + 1))


def build_program(dbg=False, dbg2=False):
    nc = bass.Bass("TRN2", target_bir_lowering=False)

    x_d = nc.dram_tensor("x", [NT, D], F32, kind="ExternalInput").ap()
    wq_d = nc.dram_tensor("wq", [D, D], F32, kind="ExternalInput").ap()
    wk_d = nc.dram_tensor("wk", [D, 128], F32, kind="ExternalInput").ap()
    wv_d = nc.dram_tensor("wv", [D, 128], F32, kind="ExternalInput").ap()
    bq_d = nc.dram_tensor("bq", [128, 32], F32, kind="ExternalInput").ap()
    bk_d = nc.dram_tensor("bk", [128], F32, kind="ExternalInput").ap()
    bv_d = nc.dram_tensor("bv", [128], F32, kind="ExternalInput").ap()
    wt_d = nc.dram_tensor("wt", [D, D], F32, kind="ExternalInput").ap()
    bb_d = nc.dram_tensor("bb", [D], F32, kind="ExternalInput").ap()
    rwt_d = nc.dram_tensor("rwt", [D, 15], F32, kind="ExternalInput").ap()
    rb_d = nc.dram_tensor("rb", [15], F32, kind="ExternalInput").ap()
    at_d = nc.dram_tensor("at", [D, 128], F32, kind="ExternalInput").ap()
    bt_d = nc.dram_tensor("bt", [128, D], F32, kind="ExternalInput").ap()
    em_d = nc.dram_tensor("emap", [15], I32, kind="ExternalInput").ap()
    y_d = nc.dram_tensor("y", [NT, D], F32, kind="ExternalOutput").ap()

    if dbg2:
        d2_xT = nc.dram_tensor("d2_xT", [128, NCH * NT], BF16, kind="ExternalOutput").ap()
        d2_hT = nc.dram_tensor("d2_hT", [128, NT], BF16, kind="ExternalOutput").ap()
        d2_lgb = nc.dram_tensor("d2_lgb", [15, NT], BF16, kind="ExternalOutput").ap()
        d2_hw = nc.dram_tensor("d2_hw", [128, NT], BF16, kind="ExternalOutput").ap()
        d2_wcb = nc.dram_tensor("d2_wcb", [128, 240], BF16, kind="ExternalOutput").ap()
        d2_qk = nc.dram_tensor("d2_qk", [TILES, 128, 256], BF16, kind="ExternalOutput").ap()

    scr_qk = nc.dram_tensor("scr_qk", [TILES, 128, 256], BF16, kind="Internal").ap()
    scr_b = nc.dram_tensor("scr_b", [128], F32, kind="Internal").ap()
    pre2_d = nc.dram_tensor("pre2", [NT, D], BF16, kind="Internal").ap()

    if dbg:
        dbg_o = nc.dram_tensor("dbg_o", [TILES, 128, 128], BF16, kind="ExternalOutput").ap()
        dbg_x1 = nc.dram_tensor("dbg_x1", [TILES, 128, D], BF16, kind="ExternalOutput").ap()
        dbg_hw = nc.dram_tensor("dbg_hw", [128, NT], BF16, kind="ExternalOutput").ap()
        dbg_wq = nc.dram_tensor("dbg_wq", [128, NCH * 384], BF16, kind="ExternalOutput").ap()
        dbg_br = nc.dram_tensor("dbg_br", [1, 384], BF16, kind="ExternalOutput").ap()
        dbg_qk = nc.dram_tensor("dbg_qk", [TILES, 128, 256], BF16, kind="ExternalOutput").ap()

    id128_h = nc.inline_tensor(np.eye(128, dtype=np.float32).astype(ml_dtypes.bfloat16), "c_id128")
    id15_h = nc.inline_tensor(np.eye(15, dtype=np.float32).astype(ml_dtypes.bfloat16), "c_id15")
    repl_np = np.zeros((128, D), np.float32)
    repl_np[np.arange(D) // 32, np.arange(D)] = 1.0
    repl_h = nc.inline_tensor(repl_np.astype(ml_dtypes.bfloat16), "c_repl")
    r16_np = np.zeros((8, 128), np.float32)
    r16_np[np.arange(128) // 16, np.arange(128)] = 1.0
    r16_h = nc.inline_tensor(r16_np.astype(ml_dtypes.bfloat16), "c_r16")
    iota_h = nc.inline_tensor(np.tile(np.arange(8, dtype=np.float32), (15, 1)), "c_iota")
    ones_h = nc.inline_tensor(np.ones((1, 128), np.float32).astype(ml_dtypes.bfloat16), "c_ones")

    st = ExitStack()
    with st:
        E = st.enter_context

        xT = E(nc.sbuf_tensor("xT", [128, NCH * NT], BF16))
        id128 = E(nc.sbuf_tensor("id128", [128, 128], BF16))
        id15 = E(nc.sbuf_tensor("id15", [15, 15], BF16))
        ones1 = E(nc.sbuf_tensor("ones1", [1, 128], BF16))
        brow = E(nc.sbuf_tensor("brow", [1, 384], BF16))
        r16 = E(nc.sbuf_tensor("r16", [8, 128], BF16))
        iota = E(nc.sbuf_tensor("iota", [15, 8], F32))
        emc = E(nc.sbuf_tensor("emc", [15, 1], I32))
        emf = E(nc.sbuf_tensor("emf", [15, 1], F32))
        ohb = E(nc.sbuf_tensor("ohb", [15, 8], BF16))
        rbc = E(nc.sbuf_tensor("rbc", [15, 1], F32))
        epsc = E(nc.sbuf_tensor("epsc", [128, 1], F32))
        hwT = E(nc.sbuf_tensor("hwT", [128, NT], BF16))

        pA = E(nc.psum_tensor("pA", [128, 512], F32))
        pB = E(nc.psum_tensor("pB", [128, 512], F32))
        pC = E(nc.psum_tensor("pC", [128, 512], F32))
        pD = E(nc.psum_tensor("pD", [128, 512], F32))
        pE = E(nc.psum_tensor("pE", [128, 1024], F32))
        pF = E(nc.psum_tensor("pF", [128, 1024], F32))

        sst = ExitStack()
        st.enter_context(sst)
        mk = lambda name: Sem(nc, sst, name)
        s_cl = mk("cl"); s_xl = mk("xl"); s_xtr = mk("xtr"); s_xev = mk("xev")
        s_wql = mk("wql"); s_wqr = mk("wqr"); s_bx = mk("bx")
        s_qkv = mk("qkv"); s_qke = mk("qke"); s_fl = mk("fl"); s_vt = mk("vt")
        s_vi = mk("vi"); s_sg = mk("sg"); s_ex = mk("ex"); s_nd = mk("nd")
        s_dv = mk("dv"); s_ln = mk("ln"); s_le = mk("le"); s_stt = mk("stt")
        s_nm = mk("nm"); s_x1 = mk("x1"); s_x1e = mk("x1e")
        s_b0m = mk("b0m"); s_b0e = mk("b0e"); s_lgt = mk("lgt"); s_pr = mk("pr")
        s_exr = mk("exr"); s_wct = mk("wct"); s_wce = mk("wce"); s_w8 = mk("w8")
        s_w8e = mk("w8e"); s_w128 = mk("w128"); s_hw = mk("hw")
        s_wl = mk("wl"); s_bl = mk("bl"); s_fm = mk("fm"); s_fe = mk("fe")
        s_fb = mk("fb"); s_sp = mk("sp"); s_p2 = mk("p2"); s_pcs = mk("pcs")
        s_pcn = mk("pcn"); s_out = mk("out"); s_ph = mk("ph"); s_cl2 = mk("cl2"); s_dbg = mk("dbg")
        s_vv = mk("vv"); s_bx1 = mk("bx1"); s_bx2 = mk("bx2"); s_b0d = mk("b0d"); s_xc = mk("xc")
        s_rmx = mk("rmx")
        ds_x = DStream(nc, sst, "dsx", 3)
        ds_wq = DStream(nc, sst, "dswq", 3)
        ds_fo = DStream(nc, sst, "dsfo")
        ds_fi = DStream(nc, sst, "dsfi")
        ds_sl = DStream(nc, sst, "dssl")
        ds_bb = DStream(nc, sst, "dsbb")
        ds_sp = DStream(nc, sst, "dssp")
        ds_p2 = DStream(nc, sst, "dsp2")
        ds_y = DStream(nc, sst, "dsy")

        def vst(v, ins):
            s_vv.inc(ins)

        def vwt(v):
            v.wait_ge(s_vv.h, s_vv.n)

        xT3 = xT[:].rearrange("p (c n) -> p c n", c=NCH)

        PH = [0]

        def barrier(blk, guards):
            PH[0] += 1
            tgt = 5 * PH[0]
            for name in ("gpsimd", "sync", "vector", "scalar", "tensor"):
                def _f(eng, _g=guards.get(name, ()), _t=tgt):
                    for sem, val in _g:
                        eng.wait_ge(sem.h, val)
                    eng.nop().then_inc(s_ph.h, 1)
                    eng.wait_ge(s_ph.h, _t)
                getattr(blk, name)(_f)
            s_ph.n = tgt

        p01 = ExitStack(); st.enter_context(p01)
        wqkv = p01.enter_context(nc.sbuf_tensor("wqkv", [128, NCH * 384], BF16))
        p0 = ExitStack(); st.enter_context(p0)
        xrows = [p0.enter_context(nc.sbuf_tensor(f"xrows{i}", [128, D], BF16)) for i in range(3)]
        wqst = [p0.enter_context(nc.sbuf_tensor(f"wqst{i}", [128, D], BF16)) for i in range(2)]
        wq3 = wqkv[:].rearrange("p (c n) -> p c n", c=NCH)
        wqcol = p0.enter_context(nc.sbuf_tensor("wqcol", [128, 128], F32))
        bqs = p0.enter_context(nc.sbuf_tensor("bqs", [128, 32], F32))
        bqc = p0.enter_context(nc.sbuf_tensor("bqc", [128, 1], F32))

        with nc.Block() as blk:
            @blk.sync
            def _(sy):
                s_cl.inc(sy.dma_start(id128[:], id128_h.ap()), 16)
                s_cl.inc(sy.dma_start(id15[:], id15_h.ap()), 16)
                s_cl.inc(sy.dma_start(ones1[:], ones_h.ap()), 16)
                s_cl.inc(sy.dma_start(r16[:], r16_h.ap()), 16)
                s_cl.inc(sy.dma_start(iota[:], iota_h.ap()), 16)
                s_cl.inc(sy.dma_start(emc[:], em_d.unsqueeze(1)), 16)
                s_cl.inc(sy.dma_start(rbc[:], rb_d.unsqueeze(1)), 16)
                s_cl.inc(sy.dma_start(bqs[:], bq_d), 16)
                sy.wait_ge(s_bx.h, 1)
                s_bx1.inc(sy.dma_start(scr_b.unsqueeze(1), bqc[:]), 16)

            @blk.gpsimd
            def _(g):
                s_cl2.inc(g.dma_start(wq3[:, :, 128:256],
                                      wk_d.rearrange("(c p) n -> p c n", p=128)), 16)
                s_cl2.inc(g.dma_start(wq3[:, :, 256:384],
                                      wv_d.rearrange("(c p) n -> p c n", p=128)), 16)
                s_cl2.inc(g.dma_start(brow[0:1, 128:256], bk_d.unsqueeze(0)), 16)
                s_cl2.inc(g.dma_start(brow[0:1, 256:384], bv_d.unsqueeze(0)), 16)
                for t in range(TILES):
                    if t >= 3:
                        g.wait_ge(s_xev.h, 4 * (t - 3) + 4)
                    ds_x.issue(g, xrows[t % 3][:], x_d[t * 128:(t + 1) * 128, :])
                    for c in (2 * t, 2 * t + 1):
                        if c >= 2:
                            g.wait_ge(s_wqr.h, c - 1)
                        ds_wq.issue(g, wqst[c % 2][:], wq_d[c * 128:(c + 1) * 128, :])
                g.wait_ge(s_bx1.h, 16)
                s_bx2.inc(g.dma_start(brow[0:1, 0:128], scr_b.unsqueeze(0)), 16)

            @blk.vector
            def _(v):
                v.memset(epsc[:], EPS)
                v.wait_ge(s_cl.h, 128)
                s_bx.inc(v.reduce_sum(bqc[:], bqs[:], axis=AX.X), 1)

                for t in range(TILES):
                    for c in (2 * t, 2 * t + 1):
                        ds_wq.wait(v, c)
                        v.wait_ge(s_wqr.h, c)
                        vst(v, v.reduce_sum(
                            wqcol[:],
                            wqst[c % 2][:].rearrange("p (d h) -> p d h", h=32),
                            axis=AX.X))
                        vwt(v)
                        s_wqr.inc(v.tensor_copy(wq3[:, c, 0:128], wqcol[:]), 1)

            @blk.tensor
            def _(t_):
                t_.wait_ge(s_cl.h, 128)
                for t in range(TILES):
                    ds_x.wait(t_, t)
                    for q in range(4):
                        pq = pE if q % 2 == 0 else pF
                        t_.wait_ge(s_xev.h, max(0, 4 * t + q - 1))
                        last = None
                        for cc in range(8):
                            c = 8 * q + cc
                            last = t_.matmul(pq[:, cc * 128:(cc + 1) * 128],
                                             xrows[t % 3][:, c * 128:(c + 1) * 128],
                                             id128[:], start=True, stop=True)
                        s_xtr.inc(last)

            @blk.scalar
            def _(s):
                for t in range(TILES):
                    for q in range(4):
                        pq = pE if q % 2 == 0 else pF
                        s.wait_ge(s_xtr.h, 4 * t + q + 1)
                        dst = xT3[:, 8 * q:8 * q + 8, t * 128:(t + 1) * 128]
                        s_xev.inc(s.activation(dst, pq[:].rearrange("p (c n) -> p c n", c=8),
                                               AF.Copy))

            barrier(blk, {
                "gpsimd": [(ds_x.sems[0], 16 * 6), (ds_wq.sems[0], 16 * 11), (s_cl2, 64), (s_bx2, 16)],
                "sync": [(s_cl, 128), (s_bx1, 16)],
                "vector": [(s_wqr, 32)],
                "scalar": [(s_xev, 64)],
                "tensor": [(s_xtr, 64)],
            })
        p0.close()

        pa = ExitStack(); st.enter_context(pa)
        repl = pa.enter_context(nc.sbuf_tensor("repl", [128, D], BF16))
        qf = pa.enter_context(nc.sbuf_tensor("qf", [128, 8192], BF16))
        vi = [pa.enter_context(nc.sbuf_tensor(f"vi{i}", [128, 256], BF16)) for i in range(2)]
        pbuf = [pa.enter_context(nc.sbuf_tensor(f"pbuf{i}", [128, 1024], BF16)) for i in range(2)]
        qkr = [pa.enter_context(nc.sbuf_tensor(f"qkr{i}", [128, 256], BF16)) for i in range(2)]
        vr = [pa.enter_context(nc.sbuf_tensor(f"vr{i}", [128, 128], BF16)) for i in range(2)]
        otb = [pa.enter_context(nc.sbuf_tensor(f"otb{i}", [128, 128], BF16)) for i in range(2)]
        rbuf = pa.enter_context(nc.sbuf_tensor("rbuf", [128, 128], F32))
        pre1 = pa.enter_context(nc.sbuf_tensor("pre1", [128, D], BF16))
        bn6 = pa.enter_context(nc.sbuf_tensor("bn6", [128, 48], F32))
        mv = pa.enter_context(nc.sbuf_tensor("mv", [128, 2], F32))
        sdc = pa.enter_context(nc.sbuf_tensor("sdc", [128, 1], F32))
        rstd = pa.enter_context(nc.sbuf_tensor("rstd", [128, 1], F32))
        nmr = pa.enter_context(nc.sbuf_tensor("nmr", [128, 1], F32))
        hlf = pa.enter_context(nc.sbuf_tensor("hlf", [128, 1], F32))

        with nc.Block() as blk:
            @blk.gpsimd
            def _(g):
                s_cl2.inc(g.dma_start(repl[:], repl_h.ap()), 16)
                g.memset(vi[0][:].rearrange("p (j two) -> p j two", two=2)[:, :, 1:2], 1.0)
                s_vi.inc(g.memset(
                    vi[1][:].rearrange("p (j two) -> p j two", two=2)[:, :, 1:2], 1.0))

            @blk.sync
            def _(sy):
                if dbg:
                    sy.wait_ge(s_wqr.h, 32)
                    sy.wait_ge(s_cl2.h, 64)
                    sy.wait_ge(s_bx.h, 33)
                    s_dbg.inc(sy.dma_start(dbg_wq, wqkv[:]), 16)
                    s_dbg.inc(sy.dma_start(dbg_br, brow[:]), 16)
                for t in range(TILES):
                    par = t % 2
                    sy.wait_ge(s_qke.h, 3 * t + 3)
                    if dbg:
                        s_dbg.inc(sy.dma_start(dbg_qk[t], qkr[par][:]), 16)
                    ds_fo.issue(sy, scr_qk[t], qkr[par][:])
                    for qq in range(4):
                        ds_fo.wait(sy, t)
                        sy.wait_ge(s_sg.h, max(0, 16 * t + 4 * qq))
                        rows = scr_qk[t][32 * qq:32 * qq + 32, :]
                        ds_fi.issue(
                            sy,
                            qf[0:1, 0:8192].rearrange("o (h p n) -> o h p n", h=2, p=32),
                            rows[:, 0:256].rearrange("p (h n) -> h p n", h=2).unsqueeze(0))
                    if dbg:
                        sy.wait_ge(s_dv.h, t + 1)
                        s_dbg.inc(sy.dma_start(dbg_o[t], otb[par][:]), 16)
                        sy.wait_ge(s_nm.h, t + 1)
                        s_dbg.inc(sy.dma_start(dbg_x1[t], pre1[:]), 16)

            @blk.tensor
            def _(t_):
                t_.wait_ge(s_cl.h, 128)
                t_.wait_ge(s_cl2.h, 80)
                t_.wait_ge(s_bx2.h, 16)
                t_.wait_ge(s_wqr.h, 32)

                def qkv_mm(t):
                    pq = pA if t % 2 == 0 else pB
                    if t >= 2:
                        t_.wait_ge(s_qke.h, 3 * (t - 2) + 3)
                    for c in range(NCH):
                        t_.matmul(pq[:, 0:384], xT3[:, c, t * 128:(t + 1) * 128],
                                  wq3[:, c, :], start=(c == 0), stop=False)
                    s_qkv.inc(t_.matmul(pq[:, 0:384], ones1[:], brow[:],
                                        start=False, stop=True))

                def pointwise(t):
                    par = t % 2
                    t_.wait_ge(s_qke.h, 3 * t + 3)
                    t_.wait_ge(s_vi.h, 1 + t)
                    s_vt.inc(t_.matmul(pC[:, par * 128:par * 128 + 128], vr[par][:],
                                       id128[:], start=True, stop=True))
                    if t >= 2:
                        t_.wait_ge(s_x1e.h, 4 * (t - 2))
                    if t >= 1:
                        t_.wait_ge(s_le.h, 4 * (t - 1))
                        t_.wait_ge(s_dv.h, t)
                    def sgrp(grp):
                        ps = pE if grp % 2 == 0 else pF
                        if grp % 4 == 0:
                            ds_fi.wait(t_, 4 * t + grp // 4)
                        t_.wait_ge(s_ex.h, max(0, 16 * t + grp - 1))
                        last = None
                        for j in range(8):
                            tok = (8 * grp + j) % 32
                            last = t_.matmul(ps[:, j * 128:(j + 1) * 128],
                                             qf[0:1, 4096 + tok * 128:4096 + (tok + 1) * 128],
                                             qf[0:1, tok * 128:(tok + 1) * 128],
                                             start=True, stop=True)
                        s_sg.inc(last)

                    def ndgrp(grp):
                        gp = grp % 2
                        t_.wait_ge(s_ex.h, 16 * t + grp + 1)
                        t_.wait_ge(s_vi.h, 2 + t)
                        last = None
                        for j in range(8):
                            tok = 8 * grp + j
                            off = par * 256 + 2 * tok
                            last = t_.matmul(pD[:, off:off + 2],
                                             pbuf[gp][:, j * 128:(j + 1) * 128],
                                             vi[par][:, 2 * tok:2 * tok + 2],
                                             start=True, stop=True)
                        s_nd.inc(last)

                    sgrp(0)
                    sgrp(1)
                    for grp in range(16):
                        ndgrp(grp)
                        if grp + 2 < 16:
                            sgrp(grp + 2)

                def ln1(t):
                    par = t % 2
                    t_.wait_ge(s_dv.h, t + 1)
                    t_.wait_ge(s_ex.h, 16 * min(t + 2, TILES))
                    if t >= 1:
                        t_.wait_ge(s_x1e.h, 4 * t)
                    for piece in range(4):
                        pp = pE if piece % 2 == 0 else pF
                        if piece >= 2:
                            t_.wait_ge(s_le.h, 4 * t + piece - 1)
                        last = None
                        for half in range(2):
                            col0 = piece * 1024 + half * 512
                            last = t_.matmul(pp[:, half * 512:half * 512 + 512],
                                             otb[par][:], repl[:, col0:col0 + 512],
                                             start=True, stop=False)
                            for cc in range(4 * half, 4 * half + 4):
                                c = piece * 8 + cc
                                last = t_.matmul(pp[:, cc * 128:(cc + 1) * 128],
                                                 xT3[:, c, t * 128:(t + 1) * 128], id128[:],
                                                 start=False, stop=(cc % 4 == 3))
                        s_ln.inc(last)

                def x1t(t, ln_guard=None):
                    t_.wait_ge(s_nm.h, t + 1)
                    if ln_guard is not None:
                        t_.wait_ge(s_le.h, 4 * ln_guard)
                    for q in range(4):
                        pq = pE if q % 2 == 0 else pF
                        t_.wait_ge(s_x1e.h, max(0, 4 * t + q - 1))
                        last = None
                        for cc in range(8):
                            c = 8 * q + cc
                            last = t_.matmul(pq[:, cc * 128:(cc + 1) * 128],
                                             pre1[:, c * 128:(c + 1) * 128], id128[:],
                                             start=True, stop=True)
                        s_x1.inc(last)

                qkv_mm(0)
                for t in range(TILES):
                    if t + 1 < TILES:
                        qkv_mm(t + 1)
                    pointwise(t)
                    if t >= 2:
                        x1t(t - 2)
                    if t >= 1:
                        ln1(t - 1)
                x1t(TILES - 2, ln_guard=TILES - 1)
                ln1(TILES - 1)
                x1t(TILES - 1, ln_guard=TILES)

            @blk.scalar
            def _(s):
                def act_evac_qkv(u):
                    par = u % 2
                    pq = pA if par == 0 else pB
                    s.wait_ge(s_qkv.h, u + 1)
                    if u >= 2:
                        ds_fo.wait(s, u - 2)
                        s.wait_ge(s_vt.h, u - 1)
                    s.activation(qkr[par][:, 0:128], pq[:, 0:128], AF.Copy, scale=SCALE_Q)
                    s.activation(qkr[par][:, 128:256], pq[:, 128:256], AF.Copy)
                    s_qke.inc(s.activation(vr[par][:], pq[:, 256:384], AF.Copy), 3)

                def act_exp(t):
                    for grp in range(16):
                        gp = grp % 2
                        ps = pE if gp == 0 else pF
                        s.wait_ge(s_sg.h, 16 * t + grp + 1)
                        s.wait_ge(s_nd.h, max(0, 16 * t + grp - 1))
                        s_ex.inc(s.activation(pbuf[gp][:], ps[:], AF.Exp))

                def act_stats(t):
                    s.wait_ge(s_stt.h, 5 * t + 1)
                    s_stt.inc(s.activation(sdc[:], mv[:, 1:2], AF.Ln, bias=epsc[:]))
                    s.wait_ge(s_stt.h, 5 * t + 3)
                    s_stt.inc(s.activation(rstd[:], hlf[:], AF.Exp))

                act_evac_qkv(0)
                for t in range(TILES):
                    if t + 1 < TILES:
                        act_evac_qkv(t + 1)
                    act_exp(t)
                    if t >= 1:
                        act_stats(t - 1)
                act_stats(TILES - 1)

            @blk.vector
            def _(v):
                def dve_pw(t):
                    par = t % 2
                    v.wait_ge(s_vt.h, t + 1)
                    if t >= 2:
                        v.wait_ge(s_nd.h, 16 * (t - 1))
                    s_vi.inc(v.tensor_copy(
                        vi[par][:].rearrange("p (j two) -> p j two", two=2)[:, :, 0:1],
                        pC[:, par * 128:par * 128 + 128].unsqueeze(2)))
                    v.wait_ge(s_nd.h, 16 * (t + 1))
                    nd3 = pD[:, par * 256:par * 256 + 256].rearrange(
                        "p (j two) -> p j two", two=2)
                    vst(v, v.reciprocal(rbuf[:], nd3[:, :, 1:2].squeeze(2)))
                    vwt(v)
                    s_dv.inc(v.tensor_tensor(otb[par][:], nd3[:, :, 0:1].squeeze(2),
                                             rbuf[:], ALU.mult))

                def dve_lnev(t):
                    for piece in range(4):
                        pp = pE if piece % 2 == 0 else pF
                        v.wait_ge(s_ln.h, 4 * t + piece + 1)
                        if t >= 1 and piece == 0:
                            v.wait_ge(s_stt.h, 5 * (t - 1) + 1)
                        s_le.inc(v.tensor_copy(pre1[:, piece * 1024:(piece + 1) * 1024],
                                               pp[:]))

                def dve_x1e(t):
                    for q in range(4):
                        v.wait_ge(s_x1.h, 4 * t + q + 1)
                        pq2 = pE if q % 2 == 0 else pF
                        dst = xT3[:, 8 * q:8 * q + 8, t * 128:(t + 1) * 128]
                        s_x1e.inc(v.tensor_copy(dst, pq2[:].rearrange("p (c n) -> p c n",
                                                                      c=8)))

                def dve_stats(t):
                    v.wait_ge(s_le.h, 4 * (t + 1))
                    if t >= 1:
                        v.wait_ge(s_nm.h, t)
                    for i in range(8):
                        vst(v, v.bn_stats(bn6[:, 6 * i:6 * i + 6],
                                          pre1[:, 512 * i:512 * i + 512]))
                    vwt(v)
                    s_stt.inc(v.bn_aggr(mv[:], bn6[:]))
                    v.wait_ge(s_stt.h, 5 * t + 2)
                    s_stt.inc(v.tensor_scalar_mul(hlf[:], sdc[:], -0.5))
                    v.wait_ge(s_stt.h, 5 * t + 4)
                    s_stt.inc(v.scalar_tensor_tensor(nmr[:], mv[:, 0:1], -1.0, rstd[:],
                                                     ALU.mult, ALU.mult))
                    v.wait_ge(s_stt.h, 5 * t + 5)
                    if t >= 1:
                        v.wait_ge(s_x1.h, 4 * t)
                    s_nm.inc(v.tensor_scalar(out=pre1[:], in0=pre1[:], scalar1=rstd[:],
                                             scalar2=nmr[:], op0=ALU.mult, op1=ALU.add))

                for t in range(TILES):
                    dve_pw(t)
                    if t >= 2:
                        dve_x1e(t - 2)
                    if t >= 1:
                        dve_lnev(t - 1)
                        dve_stats(t - 1)
                dve_x1e(TILES - 2)
                dve_lnev(TILES - 1)
                dve_stats(TILES - 1)
                dve_x1e(TILES - 1)

            barrier(blk, {
                "gpsimd": [(s_cl2, 80), (s_vi, 1)],
                "sync": [(ds_fo.sems[0], 16 * 8), (ds_fo.sems[1], 16 * 8), (ds_fi.sems[0], 16 * 32), (ds_fi.sems[1], 16 * 32)],
                "vector": [(s_stt, 5 * TILES), (s_x1e, 4 * TILES), (s_nm, TILES)],
                "scalar": [(s_ex, 16 * TILES)],
                "tensor": [(s_x1, 4 * TILES)],
            })
        pa.close()
        p01.close()

        fc = ExitStack(); st.enter_context(fc)
        slab = [fc.enter_context(nc.sbuf_tensor(f"slab{i}", [128, NCH * SLABW], BF16))
                for i in range(2)]
        btb = fc.enter_context(nc.sbuf_tensor("btb", [128, D], BF16))
        bbr = [fc.enter_context(nc.sbuf_tensor(f"bbr{i}", [1, SLABW], BF16)) for i in range(2)]

        def load_slab(g, si):
            par = si % 2
            sl3 = slab[par][:].rearrange("p (c n) -> p c n", c=NCH)
            for dd in range(4):
                src = wt_d[dd * 1024:(dd + 1) * 1024, si * SLABW:(si + 1) * SLABW]
                ds_sl.issue(g, sl3[:, 8 * dd:8 * dd + 8, :],
                            src.rearrange("(c p) n -> p c n", p=128))
            ds_bb.issue(g, bbr[par][:],
                        bb_d[si * SLABW:(si + 1) * SLABW].unsqueeze(0))

        b0 = ExitStack(); st.enter_context(b0)
        erow_a = b0.enter_context(nc.sbuf_tensor("erow_a", [128, 240], F32))
        sme_a = b0.enter_context(nc.sbuf_tensor("sme_a", [128, 16], F32))
        rcp_a = b0.enter_context(nc.sbuf_tensor("rcp_a", [128, 16], F32))
        prob_a = b0.enter_context(nc.sbuf_tensor("prob_a", [128, 240], F32))
        mbuf = b0.enter_context(nc.sbuf_tensor("mbuf", [128, 1800], BF16))
        rank_a = b0.enter_context(nc.sbuf_tensor("rank_a", [128, 240], F32))
        selm_a = b0.enter_context(nc.sbuf_tensor("selm_a", [128, 240], F32))
        wcb_a = b0.enter_context(nc.sbuf_tensor("wcb_a", [128, 240], BF16))
        wctb_a = b0.enter_context(nc.sbuf_tensor("wctb_a", [15, NT], BF16))
        w8b_a = b0.enter_context(nc.sbuf_tensor("w8b_a", [8, NT], BF16))
        nmx_a = b0.enter_context(nc.sbuf_tensor("nmx_a", [128, 16], F32))
        atb = b0.enter_context(nc.sbuf_tensor("atb", [128, NCH * 128], BF16))
        rwb = b0.enter_context(nc.sbuf_tensor("rwb", [128, NCH * 16], BF16))
        hT = b0.enter_context(nc.sbuf_tensor("hT", [128, NT], BF16))
        lgb = b0.enter_context(nc.sbuf_tensor("lgb", [15, NT], BF16))
        a3 = atb[:].rearrange("p (c n) -> p c n", c=NCH)
        r3 = rwb[:].rearrange("p (c n) -> p c n", c=NCH)

        with nc.Block() as blk:
            @blk.gpsimd
            def _(g):
                s_b0d.inc(g.dma_start(a3, at_d.rearrange("(c p) n -> p c n", p=128)), 16)
                s_b0d.inc(g.dma_start(r3[:, :, 0:15],
                                      rwt_d.rearrange("(c p) n -> p c n", p=128)), 16)
                s_cl2.inc(g.dma_start(btb[:], bt_d), 16)
                load_slab(g, 0)
                load_slab(g, 1)

            @blk.tensor
            def _(t_):
                t_.wait_ge(s_b0d.h, 32)
                for q in range(4):
                    pq = pE if q < 2 else pF
                    cols = (q % 2) * 512
                    last = None
                    for c in range(NCH):
                        last = t_.matmul(pq[:, cols:cols + 512], a3[:, c, :],
                                         xT3[:, c, q * 512:(q + 1) * 512],
                                         start=(c == 0), stop=(c == NCH - 1))
                s_b0m.inc(last)
                for q in range(4):
                    pq = (pA, pB, pC, pD)[q]
                    last = None
                    for c in range(NCH):
                        last = t_.matmul(pq[0:15, 0:512], r3[:, c, 0:15],
                                         xT3[:, c, q * 512:(q + 1) * 512],
                                         start=(c == 0), stop=(c == NCH - 1))
                s_b0m.inc(last)
                t_.wait_ge(s_b0e.h, 3)
                last = None
                for t in range(TILES):
                    last = t_.matmul(pC[:, 256 + 15 * t:271 + 15 * t],
                                     lgb[:, t * 128:(t + 1) * 128], id15[:],
                                     start=True, stop=True)
                s_lgt.inc(last)
                t_.wait_ge(s_pr.h, 2)
                for t in range(TILES):
                    pq = pC if t % 2 == 0 else pD
                    if t >= 2:
                        t_.wait_ge(s_wce.h, t - 1)
                    s_wct.inc(t_.matmul(pq[0:15, 0:128],
                                        wcb_a[:, 15 * t:15 * t + 15], id128[:],
                                        start=True, stop=True))
                t_.wait_ge(s_wce.h, TILES)
                t_.wait_ge(s_pr.h, 1)
                for half in range(2):
                    pq = pE if half == 0 else pF
                    last = None
                    for qq in range(2):
                        last = t_.matmul(pq[0:8, qq * 512:(qq + 1) * 512], ohb[:],
                                         wctb_a[:, half * 1024 + qq * 512:
                                                half * 1024 + (qq + 1) * 512],
                                         start=True, stop=True)
                    s_w8.inc(last)
                for half in range(2):
                    pq = pE if half == 0 else pF
                    t_.wait_ge(s_w8e.h, half + 1)
                    last = None
                    for qq in range(2):
                        last = t_.matmul(pq[:, qq * 512:(qq + 1) * 512], r16[:],
                                         w8b_a[0:8, half * 1024 + qq * 512:
                                               half * 1024 + (qq + 1) * 512],
                                         start=True, stop=True)
                    s_w128.inc(last)

            @blk.scalar
            def _(s):
                s.wait_ge(s_b0m.h, 1)
                s.activation(hT[:, 0:512], pE[:, 0:512], AF.Copy)
                s.activation(hT[:, 512:1024], pE[:, 512:1024], AF.Copy)
                s.activation(hT[:, 1024:1536], pF[:, 0:512], AF.Copy)
                s_b0e.inc(s.activation(hT[:, 1536:2048], pF[:, 512:1024], AF.Copy))
                s.wait_ge(s_b0m.h, 2)
                for q in range(4):
                    pq = (pA, pB, pC, pD)[q]
                    ins = s.activation(lgb[:, q * 512:(q + 1) * 512], pq[0:15, 0:512],
                                       AF.Identity, bias=rbc[:])
                s_b0e.inc(ins, 2)
                s.wait_ge(s_rmx.h, 1)
                ins = None
                for t in range(TILES):
                    ins = s.activation(erow_a[:, 15 * t:15 * t + 15],
                                       pC[:, 256 + 15 * t:271 + 15 * t], AF.Exp,
                                       bias=nmx_a[:, t:t + 1])
                s_exr.inc(ins)
                for t in range(TILES):
                    pq = pC if t % 2 == 0 else pD
                    s.wait_ge(s_wct.h, t + 1)
                    s_wce.inc(s.activation(wctb_a[:, t * 128:(t + 1) * 128],
                                           pq[0:15, 0:128], AF.Copy))
                for half in range(2):
                    pq = pE if half == 0 else pF
                    s.wait_ge(s_w8.h, half + 1)
                    s_w8e.inc(s.activation(w8b_a[0:8, half * 1024:(half + 1) * 1024],
                                           pq[0:8, 0:1024], AF.Copy, scale=2.0))

            @blk.vector
            def _(v):
                v.wait_ge(s_cl.h, 128)
                vst(v, v.tensor_copy(emf[:], emc[:]))
                vwt(v)
                vst(v, v.tensor_scalar(out=selm_a[0:15, 0:8], in0=iota[:], scalar1=emf[:],
                                       scalar2=None, op0=ALU.is_equal))
                vwt(v)
                s_pr.inc(v.tensor_copy(ohb[:], selm_a[0:15, 0:8]))
                v.wait_ge(s_lgt.h, 1)
                s_rmx.inc(v.reduce_max(
                    nmx_a[:], pC[:, 256:496].rearrange("p (t j) -> p t j", t=TILES),
                    axis=AX.X, negate=True))
                v.wait_ge(s_exr.h, 1)
                e3 = erow_a[:].rearrange("p (t j) -> p t j", t=TILES)
                vst(v, v.reduce_sum(sme_a[:], e3, axis=AX.X))
                vwt(v)
                vst(v, v.reciprocal(rcp_a[:], sme_a[:]))
                vwt(v)
                vst(v, v.tensor_tensor(
                    prob_a[:].rearrange("p (t j) -> p t j", t=TILES), e3,
                    rcp_a[:].unsqueeze(2).broadcast_to([128, 16, 15]), ALU.mult))
                vwt(v)
                for hh in range(2):
                    p3h = prob_a[:, hh * 120:(hh + 1) * 120].rearrange(
                        "p (t j) -> p t j", t=8)
                    vst(v, v.tensor_tensor(
                        mbuf[:].rearrange("p (t j k) -> p t j k", t=8, j=15),
                        p3h.unsqueeze(2).broadcast_to([128, 8, 15, 15]),
                        p3h.unsqueeze(3).broadcast_to([128, 8, 15, 15]),
                        ALU.is_gt))
                    vwt(v)
                    vst(v, v.reduce_sum(
                        rank_a[:, hh * 120:(hh + 1) * 120].rearrange(
                            "p (t j) -> p t j", t=8),
                        mbuf[:].rearrange("p (t j k) -> p t j k", t=8, j=15),
                        axis=AX.X))
                    vwt(v)
                vst(v, v.tensor_scalar(out=selm_a[:], in0=rank_a[:], scalar1=7.5,
                                       scalar2=None, op0=ALU.is_lt))
                vwt(v)
                s_pr.inc(v.tensor_tensor(wcb_a[:], prob_a[:], selm_a[:], ALU.mult))
                for half in range(2):
                    pq = pE if half == 0 else pF
                    v.wait_ge(s_w128.h, half + 1)
                    s_hw.inc(v.tensor_tensor(hwT[:, half * 1024:(half + 1) * 1024],
                                             hT[:, half * 1024:(half + 1) * 1024],
                                             pq[:], ALU.mult), 8)

            @blk.sync
            def _(sy):
                if dbg:
                    sy.wait_ge(s_hw.h, 16)
                    s_dbg.inc(sy.dma_start(dbg_hw, hwT[:]), 16)
                    sy.wait_ge(s_dbg.h, s_dbg.n)
                if dbg2:
                    sy.wait_ge(s_b0e.h, 3)
                    s_dbg.inc(sy.dma_start(d2_qk, scr_qk), 16)
                    s_dbg.inc(sy.dma_start(d2_xT, xT[:]), 16)
                    s_dbg.inc(sy.dma_start(d2_hT, hT[:]), 16)
                    s_dbg.inc(sy.dma_start(d2_lgb, lgb[:]), 16)
                    sy.wait_ge(s_pr.h, 2)
                    s_dbg.inc(sy.dma_start(d2_wcb, wcb_a[:]), 16)
                    sy.wait_ge(s_hw.h, 16)
                    s_dbg.inc(sy.dma_start(d2_hw, hwT[:]), 16)
                    sy.wait_ge(s_dbg.h, s_dbg.n)
        b0.close()

        fc2 = ExitStack(); st.enter_context(fc2)
        outb = [fc2.enter_context(nc.sbuf_tensor(f"outb{i}", [128, SLABW], BF16))
                for i in range(2)]
        bnf = fc2.enter_context(nc.sbuf_tensor("bnf", [128, TILES * 96], F32))
        pre2 = [fc2.enter_context(nc.sbuf_tensor("pre2_0", [128, D], BF16))]
        our = [fc2.enter_context(nc.sbuf_tensor(f"our{i}", [128, 2048], F32))
               for i in range(2)]
        mv2 = fc2.enter_context(nc.sbuf_tensor("mv2", [128, 2], F32))
        sd2 = fc2.enter_context(nc.sbuf_tensor("sd2", [128, 1], F32))
        rs2 = fc2.enter_context(nc.sbuf_tensor("rs2", [128, 1], F32))
        nm2 = fc2.enter_context(nc.sbuf_tensor("nm2", [128, 1], F32))
        hl2 = fc2.enter_context(nc.sbuf_tensor("hl2", [128, 1], F32))

        with nc.Block() as blk:
            @blk.gpsimd
            def _(g):
                for si in range(2, NSLAB):
                    g.wait_ge(s_fm.h, TILES * (si - 1))
                    load_slab(g, si)

            @blk.tensor
            def _(t_):
                t_.wait_ge(s_b0e.h, 3)
                t_.wait_ge(s_cl2.h, 96)
                for si in range(NSLAB):
                    par = si % 2
                    ds_sl.wait_all_upto(t_, 4 * si + 3)
                    ds_bb.wait(t_, si)
                    sl3 = slab[par][:].rearrange("p (c n) -> p c n", c=NCH)
                    for t in range(TILES):
                        k = si * TILES + t
                        pq = pA if k % 2 == 0 else pB
                        if k >= 2:
                            t_.wait_ge(s_fe.h, k - 1)
                        for c in range(NCH):
                            t_.matmul(pq[:, 0:SLABW], xT3[:, c, t * 128:(t + 1) * 128],
                                      sl3[:, c, :], start=(c == 0), stop=False)
                        for cc in range(2):
                            c = 2 * si + cc
                            t_.matmul(pq[:, cc * 128:(cc + 1) * 128],
                                      xT3[:, c, t * 128:(t + 1) * 128], id128[:],
                                      start=False, stop=False)
                        if si == 0:
                            t_.wait_ge(s_hw.h, t + 1)
                        t_.matmul(pq[:, 0:SLABW], hwT[:, t * 128:(t + 1) * 128],
                                  btb[:, si * SLABW:(si + 1) * SLABW],
                                  start=False, stop=False)
                        s_fm.inc(t_.matmul(pq[:, 0:SLABW], ones1[:], bbr[par][:],
                                           start=False, stop=True))

            @blk.scalar
            def _(s):
                s.wait_ge(s_hw.h, TILES)
                for si in range(NSLAB):
                    for t in range(TILES):
                        k = si * TILES + t
                        pq = pA if k % 2 == 0 else pB
                        s.wait_ge(s_fm.h, k + 1)
                        if k >= 2:
                            ds_sp.wait(s, k - 2)
                            s.wait_ge(s_fb.h, k - 1)
                        s_fe.inc(s.activation(outb[k % 2][:], pq[:, 0:SLABW], AF.Copy))
                        if si == NSLAB - 1:
                            s.wait_ge(s_pcs.h, 5 * t + 1)
                            s_pcs.inc(s.activation(sd2[:], mv2[:, 1:2], AF.Ln,
                                                   bias=epsc[:]))
                            s.wait_ge(s_pcs.h, 5 * t + 3)
                            s_pcs.inc(s.activation(rs2[:], hl2[:], AF.Exp))
                            for half in range(2):
                                s.wait_ge(s_pcn.h, 2 * t + half + 1)
                                ds_y.issue(s,
                                           y_d[t * 128:(t + 1) * 128,
                                               half * 2048:(half + 1) * 2048],
                                           our[(2 * t + half) % 2][:])

            @blk.vector
            def _(v):
                for si in range(NSLAB):
                    for t in range(TILES):
                        k = si * TILES + t
                        v.wait_ge(s_fe.h, k + 1)
                        s_fb.inc(v.bn_stats(bnf[:, t * 96 + si * 6:t * 96 + si * 6 + 6],
                                            outb[k % 2][:]))
                        if si == NSLAB - 1:
                            v.wait_ge(s_fb.h, k + 1)
                            if t >= 1:
                                v.wait_ge(s_pcn.h, 2 * t)
                            s_pcs.inc(v.bn_aggr(mv2[:], bnf[:, t * 96:t * 96 + 96]))
                            v.wait_ge(s_pcs.h, 5 * t + 2)
                            s_pcs.inc(v.tensor_scalar_mul(hl2[:], sd2[:], -0.5))
                            v.wait_ge(s_pcs.h, 5 * t + 4)
                            s_pcs.inc(v.scalar_tensor_tensor(nm2[:], mv2[:, 0:1], -1.0,
                                                             rs2[:], ALU.mult, ALU.mult))
                            v.wait_ge(s_pcs.h, 5 * t + 5)
                            ds_p2.wait(v, t)
                            for half in range(2):
                                hb = (2 * t + half) % 2
                                if 2 * t + half >= 2:
                                    ds_y.wait(v, 2 * t + half - 2)
                                s_pcn.inc(v.tensor_scalar(
                                    out=our[hb][:],
                                    in0=pre2[0][:, half * 2048:(half + 1) * 2048],
                                    scalar1=rs2[:], scalar2=nm2[:],
                                    op0=ALU.mult, op1=ALU.add))

            @blk.sync
            def _(sy):
                for si in range(NSLAB):
                    for t in range(TILES):
                        k = si * TILES + t
                        sy.wait_ge(s_fe.h, k + 1)
                        ds_sp.issue(sy,
                                    pre2_d[t * 128:(t + 1) * 128,
                                           si * SLABW:(si + 1) * SLABW],
                                    outb[k % 2][:])
                        if si == NSLAB - 1:
                            ds_sp.wait_all_upto(sy, (NSLAB - 1) * TILES + t)
                            if t >= 1:
                                sy.wait_ge(s_pcn.h, 2 * t)
                            ds_p2.issue(sy, pre2[0][:],
                                        pre2_d[t * 128:(t + 1) * 128, :])
                ds_y.wait_done(sy)

            barrier(blk, {
                "gpsimd": [(ds_sl.sems[0], 16 * 32), (ds_sl.sems[1], 16 * 32),
                           (ds_bb.sems[0], 16 * 8), (ds_bb.sems[1], 16 * 8)],
                "sync": [],
                "vector": [(s_pcs, 5 * TILES), (s_pcn, 2 * TILES)],
                "scalar": [(s_pcs, 5 * TILES)],
                "tensor": [(s_fm, NSLAB * TILES)],
            })
        fc2.close()
        fc.close()

        assert s_qkv.n == TILES and s_qke.n == 3 * TILES, (s_qkv.n, s_qke.n)
        assert s_sg.n == 16 * TILES and s_ex.n == 16 * TILES and s_nd.n == 16 * TILES
        assert s_dv.n == TILES
        assert s_ln.n == 4 * TILES and s_le.n == 4 * TILES
        assert s_x1.n == 4 * TILES and s_x1e.n == 4 * TILES
        assert s_stt.n == 5 * TILES and s_nm.n == TILES
        assert s_fm.n == NSLAB * TILES and s_fe.n == NSLAB * TILES
        assert s_fb.n == NSLAB * TILES
        assert s_pcs.n == 5 * TILES and s_pcn.n == 2 * TILES
        assert ds_y.i == 2 * TILES and ds_sp.i == NSLAB * TILES

    return nc


def _stage(inputs):
    f = np.ascontiguousarray
    x = np.asarray(inputs["L0_attn_x"], np.float32).reshape(16384, D)
    shared = dict(
        wq=np.asarray(inputs["L0_attn_Wq"], np.float32).reshape(D, D),
        wk=np.asarray(inputs["L0_attn_Wk"], np.float32).reshape(D, 128),
        wv=np.asarray(inputs["L0_attn_Wv"], np.float32).reshape(D, 128),
        bq=np.asarray(inputs["L0_attn_bq"], np.float32),
        bk=np.asarray(inputs["L0_attn_bk"], np.float32).reshape(128),
        bv=np.asarray(inputs["L0_attn_bv"], np.float32).reshape(128),
        wt=f(np.asarray(inputs["L0_ffn_base_W"], np.float32).T),
        bb=np.asarray(inputs["L0_ffn_base_b"], np.float32),
        rwt=f(np.asarray(inputs["L0_ffn_router_W"], np.float32).T),
        rb=np.asarray(inputs["L0_ffn_router_b"], np.float32),
        at=f(np.asarray(inputs["L0_ffn_A"], np.float32).reshape(128, D).T),
        bt=f(np.asarray(inputs["L0_ffn_B"], np.float32).transpose(0, 2, 1).reshape(128, D)),
        emap=np.asarray(inputs["L0_ffn_expert_map"], np.int32),
    )
    return x, shared


def kernel(**inputs):
    x, shared = _stage(inputs)
    nc = build_program()
    in_maps = [dict(shared, x=np.ascontiguousarray(x[c * NT:(c + 1) * NT]))
               for c in range(8)]
    res = run_bass_kernel_spmd(nc, in_maps, core_ids=list(range(8)))
    out = np.concatenate([np.asarray(res.results[c]["y"]) for c in range(8)], axis=0)
    return out.reshape(2, 8192, D).astype(np.float32)
